# version 06
# baseline (speedup 1.0000x reference)
import numpy as np
import ml_dtypes
import concourse.bass as bass
import concourse.mybir as mybir
from concourse.bass_utils import run_bass_kernel_spmd

F32 = mybir.dt.float32
BF16 = mybir.dt.bfloat16
AF = mybir.ActivationFunctionType
ALU = mybir.AluOpType
AX = mybir.AxisListType

NCORES = 8
D = 2048
NH = 16
T = 1024
TH = 1168
NE = 32
NQT = 8
SCALE = float(1.0 / np.sqrt(128.0))
BIG = 1.0e30

DC = D // 128
LC = 4
MC = 8
SC = 32


class Bld:
    STREAMS = ("tensor", "vector", "scalar", "sync", "gpsimd")

    def __init__(self, nc):
        self.nc = nc
        self.streams = {s: [] for s in self.STREAMS}
        self.semh = {}
        self.semcnt = {}
        self.known = {s: {} for s in self.STREAMS}
        self.clock_at = {}
        self.res = {}
        self.pending_alias = []

    def sem(self, name):
        if name not in self.semh:
            self.semh[name] = self.nc.alloc_semaphore(name=name)
            self.semcnt[name] = 0
        return self.semh[name]

    def new_res(self, *names):
        for n in names:
            assert n not in self.res, n
            self.res[n] = {"w": None, "r": list(self.pending_alias)}

    def retire(self, *names):
        pend = list(self.pending_alias)
        for n in names:
            r = self.res.pop(n, None)
            if r is None:
                continue
            if r["w"] is not None:
                pend.append(r["w"])
            pend.extend(r["r"])
        best = {}
        for (s, v) in pend:
            best[s] = max(best.get(s, 0), v)
        self.pending_alias = list(best.items())

    def op(self, stream, emit, reads=(), writes=(), sem=None, inc=1):
        if sem is None:
            sem = {"tensor": "pe", "vector": "ve", "scalar": "se"}[stream]
        self.sem(sem)
        need = {}

        def add(m):
            s, v = m
            if need.get(s, 0) < v:
                need[s] = v

        for r in reads:
            rr = self.res[r]
            if rr["w"] is not None:
                add(rr["w"])
        for w in writes:
            rr = self.res[w]
            if rr["w"] is not None:
                add(rr["w"])
            for m in rr["r"]:
                add(m)
        kn = self.known[stream]
        waits = []
        for s, v in sorted(need.items()):
            if kn.get(s, 0) >= v:
                continue
            waits.append((s, v))
            for ks, kv in self.clock_at.get((s, v), {}).items():
                if kn.get(ks, 0) < kv:
                    kn[ks] = kv
            kn[s] = max(kn.get(s, 0), v)
        self.semcnt[sem] += inc
        cnt = self.semcnt[sem]
        if stream == "tensor":
            kn[sem] = cnt
        m = (sem, cnt)
        ca = dict(kn)
        ca[sem] = max(ca.get(sem, 0), cnt)
        self.clock_at[m] = ca
        for r in reads:
            lst = self.res[r]["r"]
            for i, (s0, _) in enumerate(lst):
                if s0 == sem:
                    lst[i] = m
                    break
            else:
                lst.append(m)
        for w in writes:
            self.res[w] = {"w": m, "r": []}
        self.streams[stream].append((waits, emit, sem, inc))
        return m

    def member_ops(self, stream, members, writes=(), sem=None, inc=1):
        if sem is None:
            sem = {"tensor": "pe", "vector": "ve", "scalar": "se"}[stream]
        self.sem(sem)
        kn = self.known[stream]
        all_reads = []
        for i, (emit, reads) in enumerate(members):
            need = {}

            def add(mm):
                s, v = mm
                if need.get(s, 0) < v:
                    need[s] = v

            for r in reads:
                rr = self.res[r]
                if rr["w"] is not None:
                    add(rr["w"])
                all_reads.append(r)
            if i == 0:
                for w in writes:
                    rr = self.res[w]
                    if rr["w"] is not None:
                        add(rr["w"])
                    for mm in rr["r"]:
                        add(mm)
            waits = []
            for s, v in sorted(need.items()):
                if kn.get(s, 0) >= v:
                    continue
                waits.append((s, v))
                for ks, kv in self.clock_at.get((s, v), {}).items():
                    if kn.get(ks, 0) < kv:
                        kn[ks] = kv
                kn[s] = max(kn.get(s, 0), v)
            last = i == len(members) - 1
            self.streams[stream].append(
                (waits, emit, sem if last else None, inc if last else 0))
        self.semcnt[sem] += inc
        cnt = self.semcnt[sem]
        if stream == "tensor":
            kn[sem] = cnt
        m = (sem, cnt)
        ca = dict(kn)
        ca[sem] = max(ca.get(sem, 0), cnt)
        self.clock_at[m] = ca
        for r in all_reads:
            lst = self.res[r]["r"]
            for i, (s0, _) in enumerate(lst):
                if s0 == sem:
                    lst[i] = m
                    break
            else:
                lst.append(m)
        for w in writes:
            self.res[w] = {"w": m, "r": []}
        return m

    def wait_only(self, stream, sems_vals):
        self.streams[stream].append((list(sems_vals), None, None, 0))

    def drain(self, stream, sem):
        self.wait_only(stream, [(sem, self.semcnt[sem])])

    def replay(self, block):
        for sname in self.STREAMS:
            items = self.streams[sname]
            if not items:
                continue

            def make(items=items):
                def body(eng):
                    for waits, emit, sem, inc in items:
                        for (s, v) in waits:
                            eng.wait_ge(self.semh[s], v)
                        if emit is not None:
                            ins = emit(eng)
                            if sem is not None:
                                ins.then_inc(self.semh[sem], inc)
                return body

            getattr(block, sname)(make())


def build_nc(dbg=()):
    nc = bass.Bass("TRN2", target_bir_lowering=False)

    d_xkv = nc.dram_tensor("xkv", [D, TH], F32, kind="ExternalInput").ap()
    d_mask = nc.dram_tensor("mask", [NQT, 128, 272], BF16, kind="ExternalInput").ap()
    d_ident = nc.dram_tensor("ident", [128, 128], BF16, kind="ExternalInput").ap()
    d_biasb = nc.dram_tensor("biasb", [128, 32], F32, kind="ExternalInput").ap()
    d_gw = nc.dram_tensor("gw", [128, 512], F32, kind="ExternalInput").ap()
    d_wq = nc.dram_tensor("wq", [NH, 128, 2048], F32, kind="ExternalInput").ap()
    d_wk = nc.dram_tensor("wk", [NH, 128, 512], F32, kind="ExternalInput").ap()
    d_wv = nc.dram_tensor("wv", [NH, 128, 512], F32, kind="ExternalInput").ap()
    d_wkv = nc.dram_tensor("wkv", [128, 8192], F32, kind="ExternalInput").ap()
    d_wo = nc.dram_tensor("wo", [DC, 128, 2048], F32, kind="ExternalInput").ap()
    d_fc1 = nc.dram_tensor("fc1", [MC, 128, 2048], F32, kind="ExternalInput").ap()
    d_fc2 = nc.dram_tensor("fc2", [DC, 128, 1024], F32, kind="ExternalInput").ap()
    d_up = nc.dram_tensor("up", [SC, 128, 2048], F32, kind="ExternalInput").ap()
    d_dn = nc.dram_tensor("dn", [DC, 128, 4096], F32, kind="ExternalInput").ap()
    d_w1 = nc.dram_tensor("w1", [NE, 128, 8192], F32, kind="ExternalInput").ap()
    d_w2 = nc.dram_tensor("w2", [NE, 128, 8192], F32, kind="ExternalInput").ap()
    d_cwscr = nc.dram_tensor("cwscr", [NE, 1024], BF16, kind="Internal")
    d_y = nc.dram_tensor("y", [D, T], F32, kind="ExternalOutput").ap()
    d_dbg = {}
    for name, shape in dbg:
        d_dbg[name] = nc.dram_tensor("dbg_" + name, list(shape), F32,
                                     kind="ExternalOutput").ap()

    SBBASE = 16512

    def sb(name, shape, dtype, off):
        return nc.alloc_sbuf_tensor_at(name, shape, dtype, offset=SBBASE + off)

    acc = sb("acc", [128, DC * 1024], F32, 0)
    col1 = sb("col1", [128, 32], F32, 65536)
    ones_f = sb("ones_f", [128, 128], F32, 65664)
    ones_b = sb("ones_b", [128, 16], BF16, 66176)
    ident = sb("ident", [128, 128], BF16, 66208)
    gw = sb("gw", [128, 512], F32, 66464)
    biasb = sb("biasb", [128, 32], F32, 68512)
    rt = sb("rt", [128, 256], F32, 68640)
    cwtok = sb("cwtok", [128, 256], F32, 69664)
    masks = sb("masks", [128, NQT * 272], BF16, 70688)
    cwfm = sb("cwfm", [32, 1024], BF16, 75040)
    cwbf = sb("cwbf", [128, 32], BF16, 77088)
    AB = 77152
    AEND = 229376 - 16512
    x0b = sb("x0b", [128, DC * TH], BF16, AB + 0)
    kvb = sb("kvb", [128, LC * TH], BF16, AB + 37376)
    ao = sb("ao", [128, NH * 1024], BF16, AB + 46720)
    qh = [sb("qh0", [128, 1024], BF16, AB + 79488),
          sb("qh1", [128, 1024], BF16, AB + 126464)]
    kh = [sb("kh0", [128, TH], BF16, AB + 81536),
          sb("kh1", [128, TH], BF16, AB + 128512)]
    vh = [sb("vh0", [128, 1280], BF16, AB + 83872),
          sb("vh1", [128, 1280], BF16, AB + 130848)]
    sprob = [sb("sprob0", [128, 272], F32, AB + 86432),
             sb("sprob1", [128, 272], F32, AB + 133408)]
    pbf = [sb("pbf0", [128, 272], BF16, AB + 87520),
           sb("pbf1", [128, 272], BF16, AB + 134496)]
    pTe = sb("pTe", [128, 768], BF16, AB + 88064)
    wkh = [sb("wkh0", [128, 512], BF16, AB + 89600),
           sb("wkh1", [128, 512], BF16, AB + 90624)]
    wvh = [sb("wvh0", [128, 512], BF16, AB + 91648),
           sb("wvh1", [128, 512], BF16, AB + 92672)]
    wqh = [sb("wqh0", [128, 2048], BF16, AB + 93696),
           sb("wqh1", [128, 2048], BF16, AB + 97792)]
    woh = [sb("woh0", [128, 2048], BF16, AB + 101888),
           sb("woh1", [128, 2048], BF16, AB + 105984)]
    wkvb = sb("wkvb", [128, 8192], BF16, AB + 110080)
    assert AB + 135040 <= AEND, AB
    x1b = sb("x1b", [128, DC * 1024], BF16, AB + 0)
    rows = sb("rows", [1, 3072], F32, AB + 32768)
    sqbf = [sb("sqbf0", [128, 1024], BF16, AB + 45056),
            sb("sqbf1", [128, 1024], BF16, AB + 47104)]
    sq2bf = [sb("sq2bf0", [128, 1024], BF16, AB + 49152),
             sb("sq2bf1", [128, 1024], BF16, AB + 51200)]
    xlat = sb("xlat", [128, MC * 1024], BF16, AB + 53248)
    fc1s = [sb("fc1s0", [128, 2048], BF16, AB + 32768),
            sb("fc1s1", [128, 2048], BF16, AB + 36864)]
    ups = [sb("ups0", [128, 2048], BF16, AB + 32768),
           sb("ups1", [128, 2048], BF16, AB + 36864)]
    dns = [sb("dns0", [128, 1024], BF16, AB + 40960),
           sb("dns1", [128, 1024], BF16, AB + 43008)]
    interb = sb("interb", [128, 8 * 1024], BF16, AB + 69632)
    ubuf = [sb("ubuf0", [128, 512], F32, AB + 102400),
            sb("ubuf1", [128, 512], F32, AB + 104448)]
    h2b = [sb("h2b0", [128, 512], BF16, AB + 106496),
           sb("h2b1", [128, 512], BF16, AB + 107520)]
    yacc = sb("yacc", [128, MC * 1024], F32, AB + 0)
    hbf = sb("hbf", [128, MC * 1024], BF16, AB + 36864)
    cwb = [sb("cwb0", [128, 1024], BF16, AB + 108544),
           sb("cwb1", [128, 1024], BF16, AB + 110592)]
    w1s = [sb("w1s0", [128, 8192], BF16, AB + 86016),
           sb("w1s1", [128, 8192], BF16, AB + 69632)]
    w2s = sb("w2s", [128, 8192], BF16, AB + 112640)
    assert AB + 129024 <= AEND
    ylat = sb("ylat", [128, MC * 1024], BF16, AB + 36864)
    fc2s = [sb("fc2s0", [128, 1024], BF16, AB + 108544),
            sb("fc2s1", [128, 1024], BF16, AB + 110592)]
    outst = [sb("outst0", [128, 1024], F32, AB + 53248),
             sb("outst1", [128, 1024], F32, AB + 57344)]
    rows2 = sb("rows2", [1, 3072], F32, AB + 69632)
    sqbf2 = [sb("sqbf20", [128, 1024], BF16, AB + 81920),
             sb("sqbf21", [128, 1024], BF16, AB + 83968)]
    sq2bf2 = [sb("sq2bf20", [128, 1024], BF16, AB + 86016),
              sb("sq2bf21", [128, 1024], BF16, AB + 88064)]

    pA = nc.alloc_psum_tensor("pA", [128, 1024], F32)
    pB = nc.alloc_psum_tensor("pB", [128, 1024], F32)
    pC = nc.alloc_psum_tensor("pC", [128, 1024], F32)
    pT1 = nc.alloc_psum_tensor("pT1", [128, 512], BF16)
    pT2 = nc.alloc_psum_tensor("pT2", [128, 512], BF16)
    PS = {"pA": pA, "pB": pB, "pC": pC}

    def pwin(rn):
        t = PS[rn[:2]]
        half = int(rn[-1])
        return t[:, half * 512:(half + 1) * 512]

    B = Bld(nc)
    B.new_res(*[f"a{i}" for i in range(DC)])
    B.new_res("col1", "ones_f", "ones_b", "ident", "gw", "biasb", "rt",
              "cwtok", "masks", "cwfm", "cwbf",
              "x0b", "kvb", "ao", "qh0", "qh1", "kh0", "kh1", "vh0", "vh1",
              "sprob0", "sprob1", "pbf0", "pbf1", "colp0", "colp1", "pTe",
              "wkh0", "wkh1", "wvh0", "wvh1", "wqh0", "wqh1", "woh0", "woh1",
              "wkvb",
              "pA.0", "pA.1", "pB.0", "pB.1", "pC.0", "pC.1", "pT1", "pT2",
              "cwscr")

    def dma(stream, sem, out_ap, in_ap, reads=(), writes=()):
        def emit(e, o=out_ap, i=in_ap):
            return e.dma_start(o, i)
        return B.op(stream, emit, reads=reads, writes=writes, sem=sem, inc=16)

    def mmg(mms, reads, writes):
        def emit(t, mms=mms):
            ins = None
            for (o, l, r, st, sp) in mms:
                ins = t.matmul(o, l, r, start=st, stop=sp)
            return ins
        return B.op("tensor", emit, reads=reads, writes=writes)

    def scpy(out_ap, in_ap, reads, writes, func=AF.Copy, scale=1.0, bias=0.0):
        def emit(s, o=out_ap, i=in_ap, f=func, sc=scale, b=bias):
            return s.activation(o, i, f, bias=b, scale=sc)
        return B.op("scalar", emit, reads=reads, writes=writes)

    def vop(fn, reads, writes):
        return B.op("vector", fn, reads=reads, writes=writes)

    s_in_h = B.sem("s_in")

    def emit_in(e):
        e.dma_start(acc[:, :].rearrange("p (m t) -> p m t", m=DC),
                    d_xkv[:, 128:1152].rearrange("(m p) t -> p m t", p=128)
                    ).then_inc(s_in_h, 16)
        e.dma_start(masks[:, :].rearrange("p (j k) -> p j k", j=NQT),
                    d_mask.rearrange("j p k -> p j k")).then_inc(s_in_h, 16)
        e.dma_start(ident[:, :], d_ident).then_inc(s_in_h, 16)
        e.dma_start(biasb[:, :], d_biasb).then_inc(s_in_h, 16)
        return e.dma_start(gw[:, :], d_gw)
    B.semcnt["s_in"] += 4 * 16
    B.op("sync", emit_in,
         writes=tuple(f"a{i}" for i in range(DC)) + ("masks", "ident", "biasb", "gw"),
         sem="s_in", inc=16)

    dma("gpsimd", "s_x0b", x0b[:, :].rearrange("p (m t) -> p m t", m=DC),
        d_xkv.rearrange("(m p) t -> p m t", p=128), writes=("x0b",))
    dma("gpsimd", "s_wkv", wkvb[:, :], d_wkv, writes=("wkvb",))
    dma("gpsimd", "s_wq0", wqh[0][:, :], d_wq[0], writes=("wqh0",))
    dma("gpsimd", "s_wk0", wkh[0][:, :], d_wk[0], writes=("wkh0",))
    dma("gpsimd", "s_wv0", wvh[0][:, :], d_wv[0], writes=("wvh0",))

    vop(lambda v: v.memset(ones_f[:, :], 1.0), (), ("ones_f",))
    vop(lambda v: v.memset(ones_b[:, :], 1.0), (), ("ones_b",))
    vop(lambda v: v.memset(col1[:, 6:7], 1e-5), (), ("col1",))
    vop(lambda v: v.memset(col1[:, 7:8], -20.0), (), ("col1",))
    vop(lambda v: v.memset(rt[:, 76:80], -BIG), (), ("rt",))

    KWIN = [(0, 512), (512, 1024), (1024, TH)]
    rot6 = ["pA.0", "pA.1", "pB.0", "pB.1", "pC.0", "pC.1"]
    ri = 0
    for m in range(LC):
        for (w0, w1) in KWIN:
            rn = rot6[ri % 6]
            ri += 1
            mms = [(pwin(rn)[:, 0:w1 - w0],
                    wkvb[:, (fc * LC + m) * 128:(fc * LC + m) * 128 + 128],
                    x0b[:, fc * TH + w0:fc * TH + w1],
                    fc == 0, fc == DC - 1) for fc in range(DC)]
            mmg(mms, reads=("wkvb", "x0b"), writes=(rn,))
            scpy(kvb[:, m * TH + w0:m * TH + w1], pwin(rn)[:, 0:w1 - w0],
                 reads=(rn,), writes=("kvb",))

    def proj_tasks(h):
        sl = h % 2
        tasks = []

        def ktask(wi, w0, w1, sl=sl):
            rn = ["pA.0", "pA.1", "pA.0"][wi]
            mms = [(pwin(rn)[:, 0:w1 - w0],
                    wkh[sl][:, lc * 128:(lc + 1) * 128],
                    kvb[:, lc * TH + w0:lc * TH + w1],
                    lc == 0, lc == LC - 1) for lc in range(LC)]
            mmg(mms, reads=(f"wkh{sl}", "kvb"), writes=(rn,))
            scpy(kh[sl][:, w0:w1], pwin(rn)[:, 0:w1 - w0], reads=(rn,),
                 writes=(f"kh{sl}",))

        for wi, (w0, w1) in enumerate(KWIN):
            tasks.append(lambda wi=wi, w0=w0, w1=w1: ktask(wi, w0, w1))

        def vtask(tt, sl=sl):
            t0 = tt * 128
            tn = min(128, TH - t0)
            rn = ["pA.0", "pA.1"][tt % 2]
            mms = [(pwin(rn)[0:tn, 0:128],
                    kvb[:, lc * TH + t0:lc * TH + t0 + tn],
                    wvh[sl][:, lc * 128:(lc + 1) * 128],
                    lc == 0, lc == LC - 1) for lc in range(LC)]
            mmg(mms, reads=("kvb", f"wvh{sl}"), writes=(rn,))
            scpy(vh[sl][0:tn, t0:t0 + 128], pwin(rn)[0:tn, 0:128], reads=(rn,),
                 writes=(f"vh{sl}",))

        for tt in range(10):
            tasks.append(lambda tt=tt: vtask(tt))

        def qtask(wi, sl=sl):
            rn = ["pA.0", "pA.1"][wi]
            w0 = wi * 512
            mms = [(pwin(rn),
                    wqh[sl][:, fc * 128:(fc + 1) * 128],
                    x0b[:, fc * TH + 128 + w0:fc * TH + 128 + w0 + 512],
                    fc == 0, fc == DC - 1) for fc in range(DC)]
            mmg(mms, reads=(f"wqh{sl}", "x0b"), writes=(rn,))
            scpy(qh[sl][:, w0:w0 + 512], pwin(rn), reads=(rn,),
                 writes=(f"qh{sl}",), scale=SCALE)

        for wi in range(2):
            tasks.append(lambda wi=wi: qtask(wi))
        return tasks

    def emit_scores(h, qt):
        sl = h % 2
        rn = ["pB.0", "pB.1"][qt % 2]
        mms = [(pwin(rn)[:, 0:256], qh[sl][:, qt * 128:(qt + 1) * 128],
                kh[sl][:, qt * 128:qt * 128 + 256], True, True),
               (pwin(rn)[:, 256:272], qh[sl][:, qt * 128:(qt + 1) * 128],
                kh[sl][:, 1152:1168], True, True)]
        mmg(mms, reads=(f"qh{sl}", f"kh{sl}"), writes=(rn,))

    def emit_softmaxA(h, qt):
        p = qt % 2
        rn = ["pB.0", "pB.1"][p]
        sp = sprob[p]
        scpy(sp[:, :], pwin(rn)[:, 0:272], reads=(rn, "col1"),
             writes=(f"sprob{p}",), func=AF.Exp, bias=col1[:, 7:8])

    def emit_softmaxB(h, qt):
        p = qt % 2
        sp, pb_, cr = sprob[p], pbf[p], f"colp{p}"
        c0 = 3 * p
        vop(lambda v, qt=qt, sp=sp, c0=c0: v.scalar_tensor_tensor(
            sp[:, :], sp[:, :], 1.0,
            masks[:, qt * 272:(qt + 1) * 272], ALU.mult, ALU.mult,
            accum_out=col1[:, c0 + 1:c0 + 2]),
            reads=(f"sprob{p}", "masks"), writes=(f"sprob{p}", cr))
        vop(lambda v, c0=c0: v.reciprocal(col1[:, c0 + 2:c0 + 3],
                                          col1[:, c0 + 1:c0 + 2]),
            reads=(cr,), writes=(cr,))
        vop(lambda v, sp=sp, pb_=pb_, c0=c0: v.tensor_single_scalar(
            pb_[:, :], sp[:, :], col1[:, c0 + 2:c0 + 3], ALU.mult),
            reads=(f"sprob{p}", cr), writes=(f"pbf{p}",))

    def emit_avpart(h, qt):
        sl = h % 2
        p = qt % 2
        pb_ = pbf[p]
        ptn = ["pT1", "pT2"][p]
        pt = {"pT1": pT1, "pT2": pT2}[ptn]
        off = 384 * p

        def emt(t, pt=pt, pb_=pb_):
            t.transpose(pt[:, 0:128], pb_[:, 0:128], ident[:, :])
            t.transpose(pt[:, 128:256], pb_[:, 128:256], ident[:, :])
            return t.transpose(pt[0:16, 256:384], pb_[:, 256:272], ident[:, :])
        B.op("tensor", emt, reads=(f"pbf{p}", "ident"), writes=(ptn,))

        def eev(s, pt=pt, off=off):
            s.activation(pTe[:, off:off + 256], pt[:, 0:256], AF.Copy)
            return s.activation(pTe[0:16, off + 256:off + 384],
                                pt[0:16, 256:384], AF.Copy)
        B.op("scalar", eev, reads=(ptn,), writes=("pTe",))

        rn = ["pC.0", "pC.1"][qt % 2]
        mms = [(pwin(rn)[:, 0:128], vh[sl][:, qt * 128:(qt + 1) * 128],
                pTe[:, off:off + 128], True, False),
               (pwin(rn)[:, 0:128], vh[sl][:, (qt + 1) * 128:(qt + 2) * 128],
                pTe[:, off + 128:off + 256], False, False),
               (pwin(rn)[:, 0:128], vh[sl][0:16, 1152:1280],
                pTe[0:16, off + 256:off + 384], False, True)]
        mmg(mms, reads=(f"vh{sl}", "pTe"), writes=(rn,))
        scpy(ao[:, h * 1024 + qt * 128:h * 1024 + (qt + 1) * 128],
             pwin(rn)[:, 0:128], reads=(rn,), writes=("ao",))

    for t_ in proj_tasks(0):
        t_()
    for h in range(NH):
        if h + 1 < NH:
            nsl = (h + 1) % 2
            dma("gpsimd", f"s_wq{nsl}", wqh[nsl][:, :], d_wq[h + 1],
                writes=(f"wqh{nsl}",))
            dma("gpsimd", f"s_wk{nsl}", wkh[nsl][:, :], d_wk[h + 1],
                writes=(f"wkh{nsl}",))
            dma("gpsimd", f"s_wv{nsl}", wvh[nsl][:, :], d_wv[h + 1],
                writes=(f"wvh{nsl}",))
            nxt = proj_tasks(h + 1)
        else:
            nxt = []
        for qt in range(NQT):
            emit_scores(h, qt)
            emit_softmaxA(h, qt)
            if qt > 0:
                emit_softmaxB(h, qt - 1)
                emit_avpart(h, qt - 1)
            for _ in range(2):
                if nxt:
                    nxt.pop(0)()
        emit_softmaxB(h, NQT - 1)
        emit_avpart(h, NQT - 1)
        while nxt:
            nxt.pop(0)()

    dma("gpsimd", "s_wo0", woh[0][:, :], d_wo[0], writes=("woh0",))
    for m in range(DC):
        sl = m % 2
        if m + 1 < DC:
            nsl = (m + 1) % 2
            dma("gpsimd", f"s_wo{nsl}", woh[nsl][:, :], d_wo[m + 1],
                writes=(f"woh{nsl}",))
        for wi in range(2):
            rn = ["pA.0", "pA.1"][wi]
            w0 = wi * 512
            mms = [(pwin(rn),
                    woh[sl][:, c * 128:(c + 1) * 128],
                    ao[:, c * 1024 + w0:c * 1024 + w0 + 512],
                    c == 0, c == DC - 1) for c in range(DC)]
            mmg(mms, reads=(f"woh{sl}", "ao"), writes=(rn,))
            vop(lambda v, m=m, w0=w0, rn=rn: v.tensor_tensor(
                acc[:, m * 1024 + w0:m * 1024 + w0 + 512],
                acc[:, m * 1024 + w0:m * 1024 + w0 + 512], pwin(rn), ALU.add),
                reads=(rn, f"a{m}"), writes=(f"a{m}",))

    if "kvb" in d_dbg:
        dma("gpsimd", "s_dbgk", d_dbg["kvb"], kvb[:, :], reads=("kvb",))
        B.drain("gpsimd", "s_dbgk")
    if "ao" in d_dbg:
        dma("gpsimd", "s_dbga", d_dbg["ao"], ao[:, :], reads=("ao",))
        B.drain("gpsimd", "s_dbga")
    if "pre1" in d_dbg:
        dma("gpsimd", "s_dbgp", d_dbg["pre1"], acc[:, :],
            reads=tuple(f"a{i}" for i in range(DC)))
        B.drain("gpsimd", "s_dbgp")

    B.retire("x0b", "kvb", "ao", "qh", "kh", "vh", "sprob", "pbf", "pTe",
             "wkh0", "wkh1", "wvh0", "wvh1", "wqh0", "wqh1", "woh0", "woh1",
             "wkvb")

    def layernorm(rows_t, rows_rn, sqp, sq2p, sqrn, sq2rn, apply_fn):
        statbank = ["pA.0", "pA.1", "pB.0", "pB.1"]
        for m in range(DC):
            sl = m % 2
            if sl == 0:
                scpy(sqp[sl][:, :], acc[:, m * 1024:(m + 1) * 1024],
                     reads=(f"a{m}",), writes=(sqrn + str(sl),))
            else:
                vop(lambda v, m=m, sl=sl: v.tensor_copy(
                    sqp[sl][:, :], acc[:, m * 1024:(m + 1) * 1024]),
                    reads=(f"a{m}",), writes=(sqrn + str(sl),))
            for wi in range(2):
                rn = statbank[wi]
                mmg([(pwin(rn)[0:1, 0:512], ones_b[:, 0:1],
                      sqp[sl][:, wi * 512:wi * 512 + 512], m == 0, m == DC - 1)],
                    reads=(sqrn + str(sl), "ones_b"), writes=(rn,))
            if sl == 0:
                scpy(sq2p[sl][:, :], sqp[sl][:, :],
                     reads=(sqrn + str(sl),), writes=(sq2rn + str(sl),),
                     func=AF.Square)
            else:
                vop(lambda v, sl=sl: v.tensor_mul(
                    sq2p[sl][:, :], sqp[sl][:, :], sqp[sl][:, :]),
                    reads=(sqrn + str(sl),), writes=(sq2rn + str(sl),))
            for wi in range(2):
                rn = statbank[2 + wi]
                mmg([(pwin(rn)[0:1, 0:512], ones_b[:, 0:1],
                      sq2p[sl][:, wi * 512:wi * 512 + 512], m == 0, m == DC - 1)],
                    reads=(sq2rn + str(sl), "ones_b"), writes=(rn,))
        vop(lambda v: v.tensor_copy(rows_t[0:1, 0:1024], pA[0:1, 0:1024]),
            reads=("pA.0", "pA.1"), writes=(rows_rn,))
        vop(lambda v: v.scalar_tensor_tensor(
            rows_t[0:1, 2048:3072], rows_t[0:1, 0:1024], 1.0 / (D * D),
            rows_t[0:1, 0:1024], ALU.mult, ALU.mult),
            reads=(rows_rn,), writes=(rows_rn,))
        vop(lambda v: v.scalar_tensor_tensor(
            rows_t[0:1, 1024:2048], pB[0:1, 0:1024], 1.0 / D,
            rows_t[0:1, 2048:3072], ALU.mult, ALU.subtract),
            reads=("pB.0", "pB.1", rows_rn), writes=(rows_rn,))
        scpy(rows_t[0:1, 1024:2048], rows_t[0:1, 1024:2048],
             reads=(rows_rn, "col1"), writes=(rows_rn,), func=AF.Ln,
             bias=col1[0:1, 6:7])
        scpy(rows_t[0:1, 1024:2048], rows_t[0:1, 1024:2048], reads=(rows_rn,),
             writes=(rows_rn,), func=AF.Exp, scale=-0.5)
        vop(lambda v: v.scalar_tensor_tensor(
            rows_t[0:1, 2048:3072], rows_t[0:1, 0:1024], -1.0 / D,
            rows_t[0:1, 1024:2048], ALU.mult, ALU.mult),
            reads=(rows_rn,), writes=(rows_rn,))
        for wi in range(2):
            rb = ["pC.0", "pC.1"][wi]
            mb = ["pA.0", "pA.1"][wi]
            mmg([(pwin(rb), ones_f[0:1, 0:128],
                  rows_t[0:1, 1024 + wi * 512:1024 + wi * 512 + 512], True, True)],
                reads=(rows_rn, "ones_f"), writes=(rb,))
            mmg([(pwin(mb), ones_f[0:1, 0:128],
                  rows_t[0:1, 2048 + wi * 512:2048 + wi * 512 + 512], True, True)],
                reads=(rows_rn, "ones_f"), writes=(mb,))
        for m in range(DC):
            apply_fn(m)

    B.new_res("rows", "sqbf0", "sqbf1", "sq2bf0", "sq2bf1",
              *[f"xb{i}" for i in range(DC)])

    def ln1_apply(m):
        vop(lambda v, m=m: v.tensor_tensor(
            acc[:, m * 1024:(m + 1) * 1024],
            acc[:, m * 1024:(m + 1) * 1024], pC[:, 0:1024], ALU.mult),
            reads=("pC.0", "pC.1", f"a{m}"), writes=(f"a{m}",))
        vop(lambda v, m=m: v.tensor_tensor(
            acc[:, m * 1024:(m + 1) * 1024],
            acc[:, m * 1024:(m + 1) * 1024], pA[:, 0:1024], ALU.add),
            reads=("pA.0", "pA.1", f"a{m}"), writes=(f"a{m}",))
        scpy(x1b[:, m * 1024:(m + 1) * 1024], acc[:, m * 1024:(m + 1) * 1024],
             reads=(f"a{m}",), writes=(f"xb{m}",))

    layernorm(rows, "rows", sqbf, sq2bf, "sqbf", "sq2bf", ln1_apply)
    B.retire("rows", "sqbf0", "sqbf1", "sq2bf0", "sq2bf1")

    if "x1" in d_dbg:
        dma("gpsimd", "s_dbg1", d_dbg["x1"], acc[:, :],
            reads=tuple(f"a{i}" for i in range(DC)))
        B.drain("gpsimd", "s_dbg1")

    for tt in range(NQT):
        grn = ["pA.0", "pB.0"][tt % 2]
        members = []
        for fc in range(DC):
            def gmm(t, fc=fc, grn=grn, tt=tt):
                return t.matmul(
                    pwin(grn)[:, 0:32],
                    acc[:, fc * 1024 + tt * 128:fc * 1024 + (tt + 1) * 128],
                    gw[:, fc * 32:(fc + 1) * 32],
                    start=(fc == 0), stop=(fc == DC - 1))
            members.append((gmm, (f"a{fc}", "gw")))
        B.member_ops("tensor", members, writes=(grn,))
        scpy(rt[:, 200:232], pwin(grn)[:, 0:32], reads=(grn,), writes=("rt",),
             func=AF.Exp, scale=-1.0)
        vop(lambda v: v.tensor_scalar_add(rt[:, 200:232], rt[:, 200:232], 1.0),
            reads=("rt",), writes=("rt",))
        vop(lambda v: v.reciprocal(rt[:, 0:32], rt[:, 200:232]),
            reads=("rt",), writes=("rt",))
        vop(lambda v: v.tensor_add(rt[:, 32:64], rt[:, 0:32], biasb[:, :]),
            reads=("rt", "biasb"), writes=("rt",))
        for g in range(4):
            vop(lambda v, g=g: v.max(rt[:, 64:72], rt[:, 32 + 8 * g:40 + 8 * g]),
                reads=("rt",), writes=("rt",))
            vop(lambda v, g=g: v.tensor_add(rt[:, 72 + g:73 + g], rt[:, 64:65],
                                            rt[:, 65:66]),
                reads=("rt",), writes=("rt",))
        vop(lambda v: v.max(rt[:, 80:88], rt[:, 72:80]), reads=("rt",), writes=("rt",))
        vop(lambda v: v.tensor_single_scalar(rt[:, 88:92], rt[:, 72:76],
                                             rt[:, 81:82], ALU.is_ge),
            reads=("rt",), writes=("rt",))
        vop(lambda v: v.tensor_scalar(rt[:, 92:96], rt[:, 88:92], 1.0, BIG,
                                      ALU.subtract, ALU.mult),
            reads=("rt",), writes=("rt",))
        for g in range(4):
            vop(lambda v, g=g: v.tensor_single_scalar(
                rt[:, 96 + 8 * g:104 + 8 * g], rt[:, 32 + 8 * g:40 + 8 * g],
                rt[:, 92 + g:93 + g], ALU.add),
                reads=("rt",), writes=("rt",))
        vop(lambda v: v.max(rt[:, 128:136], rt[:, 96:128]), reads=("rt",),
            writes=("rt",))
        vop(lambda v: v.tensor_single_scalar(rt[:, 136:168], rt[:, 96:128],
                                             rt[:, 135:136], ALU.is_ge),
            reads=("rt",), writes=("rt",))
        vop(lambda v: v.tensor_mul(rt[:, 168:200], rt[:, 0:32], rt[:, 136:168]),
            reads=("rt",), writes=("rt",))
        vop(lambda v: v.reduce_sum(col1[:, 8:9], rt[:, 168:200], axis=AX.X),
            reads=("rt",), writes=("col1",))
        vop(lambda v: v.reciprocal(col1[:, 9:10], col1[:, 8:9]),
            reads=("col1",), writes=("col1",))
        vop(lambda v, tt=tt: v.tensor_scalar(cwtok[:, 32 * tt:32 * (tt + 1)],
                                             rt[:, 168:200], col1[:, 9:10], 2.5,
                                             ALU.mult, ALU.mult),
            reads=("rt", "col1"), writes=("cwtok",))
    if "cwtok" in d_dbg:
        dma("gpsimd", "s_dbg2", d_dbg["cwtok"], cwtok[:, :], reads=("cwtok",))
        B.drain("gpsimd", "s_dbg2")

    B.new_res("xlat", "fc1s0", "fc1s1")
    dma("gpsimd", "s_fc10", fc1s[0][:, :], d_fc1[0], writes=("fc1s0",))
    for m in range(MC):
        sl = m % 2
        if m + 1 < MC:
            nsl = (m + 1) % 2
            dma("gpsimd", f"s_fc1{nsl}", fc1s[nsl][:, :], d_fc1[m + 1],
                writes=(f"fc1s{nsl}",))
        for wi in range(2):
            rn = ["pA.0", "pA.1"][wi]
            w0 = wi * 512
            members = []
            for c in range(DC):
                def fmm(t, c=c, w0=w0, sl=sl, rn=rn):
                    return t.matmul(pwin(rn),
                                    fc1s[sl][:, c * 128:(c + 1) * 128],
                                    x1b[:, c * 1024 + w0:c * 1024 + w0 + 512],
                                    start=(c == 0), stop=(c == DC - 1))
                members.append((fmm, (f"fc1s{sl}", f"xb{c}")))
            B.member_ops("tensor", members, writes=(rn,))
            scpy(xlat[:, m * 1024 + w0:m * 1024 + w0 + 512], pwin(rn),
                 reads=(rn,), writes=("xlat",))
    for tt in range(NQT):
        ptn = ["pT1", "pT2"][tt % 2]
        pt = {"pT1": pT1, "pT2": pT2}[ptn]
        vop(lambda v, tt=tt: v.tensor_copy(cwbf[:, :],
                                           cwtok[:, 32 * tt:32 * (tt + 1)]),
            reads=("cwtok",), writes=("cwbf",))

        def emt(t, pt=pt):
            return t.transpose(pt[0:32, 0:128], cwbf[:, :], ident[:, :])
        B.op("tensor", emt, reads=("cwbf", "ident"), writes=(ptn,))
        vop(lambda v, pt=pt, tt=tt: v.tensor_copy(
            cwfm[:, tt * 128:(tt + 1) * 128], pt[0:32, 0:128]),
            reads=(ptn,), writes=("cwfm",))
    dma("sync", "s_cwo", d_cwscr.ap(), cwfm[:, :], reads=("cwfm",),
        writes=("cwscr",))

    if "xlat" in d_dbg:
        dma("gpsimd", "s_dbg3", d_dbg["xlat"], xlat[:, :], reads=("xlat",))
        B.drain("gpsimd", "s_dbg3")

    B.retire("fc1s0", "fc1s1")
    B.new_res("ups0", "ups1", "dns0", "dns1", "interb", "ubuf0", "ubuf1",
              "h2b0", "h2b1", "w1s0", "w2s", "cwb0", "cwb1")
    for S4 in range(4):
        dma("gpsimd", "s_ups0", ups[0][:, :], d_up[8 * S4], writes=("ups0",))
        for mi in range(8):
            gi = 8 * S4 + mi
            sl = mi % 2
            if mi + 1 < 8:
                nsl = (mi + 1) % 2
                dma("gpsimd", f"s_ups{nsl}", ups[nsl][:, :], d_up[gi + 1],
                    writes=(f"ups{nsl}",))
            for wi in range(2):
                rn = ["pA.0", "pA.1"][wi]
                w0 = wi * 512
                mms = [(pwin(rn),
                        ups[sl][:, c * 128:(c + 1) * 128],
                        x1b[:, c * 1024 + w0:c * 1024 + w0 + 512],
                        c == 0, c == DC - 1) for c in range(DC)]
                mmg(mms, reads=(f"ups{sl}",) + tuple(f"xb{i}" for i in range(DC)),
                    writes=(rn,))
                usl = (mi * 2 + wi) % 2
                scpy(ubuf[usl][:, :], pwin(rn), reads=(rn,),
                     writes=(f"ubuf{usl}",), func=AF.Relu)
                scpy(interb[:, mi * 1024 + w0:mi * 1024 + w0 + 512],
                     ubuf[usl][:, :], reads=(f"ubuf{usl}",), writes=("interb",),
                     func=AF.Square)
        dma("gpsimd", "s_dns0", dns[0][:, :],
            d_dn[0][:, 1024 * S4:1024 * S4 + 1024], writes=("dns0",))
        for mo in range(DC):
            sl = mo % 2
            if mo + 1 < DC:
                nsl = (mo + 1) % 2
                dma("gpsimd", f"s_dns{nsl}", dns[nsl][:, :],
                    d_dn[mo + 1][:, 1024 * S4:1024 * S4 + 1024],
                    writes=(f"dns{nsl}",))
            rny = ["pB.0", "pC.0"][mo % 2]
            rny2 = ["pB.1", "pC.1"][mo % 2]
            mms = []
            for cl in range(8):
                mms.append((pwin(rny), dns[sl][:, cl * 128:(cl + 1) * 128],
                            interb[:, cl * 1024 + 0:cl * 1024 + 512],
                            cl == 0, cl == 7))
            for cl in range(8):
                mms.append((pwin(rny2), dns[sl][:, cl * 128:(cl + 1) * 128],
                            interb[:, cl * 1024 + 512:cl * 1024 + 1024],
                            cl == 0, cl == 7))
            mmg(mms, reads=(f"dns{sl}", "interb"), writes=(rny, rny2))
            pfull = PS[rny[:2]]
            vop(lambda v, mo=mo, pfull=pfull: v.tensor_tensor(
                acc[:, mo * 1024:(mo + 1) * 1024],
                acc[:, mo * 1024:(mo + 1) * 1024], pfull[:, :], ALU.add),
                reads=(rny, rny2, f"a{mo}"), writes=(f"a{mo}",))
        if S4 == 1:
            dma("gpsimd", "s_w10", w1s[0][:, :], d_w1[0], writes=("w1s0",))
            dma("sync", "s_cwb0", cwb[0][:, :],
                bass.AP(d_cwscr, 0, [[0, 128], [1, 1024]]), reads=("cwscr",),
                writes=("cwb0",))
        elif S4 == 2:
            dma("gpsimd", "s_w2", w2s[:, :], d_w2[0], writes=("w2s",))

    B.retire("ups0", "ups1", "dns0", "dns1", "interb",
             *[f"xb{i}" for i in range(DC)])

    B.new_res("yacc", "w1s1", *[f"hbf.{i}" for i in range(MC)])
    for e in range(NE):
        sl = e % 2
        if e + 1 < NE:
            nsl = (e + 1) % 2
            dma("gpsimd", f"s_w1{nsl}", w1s[nsl][:, :], d_w1[e + 1],
                writes=(f"w1s{nsl}",))
            dma("sync", f"s_cwb{nsl}", cwb[nsl][:, :],
                bass.AP(d_cwscr, (e + 1) * 1024, [[0, 128], [1, 1024]]),
                reads=("cwscr",), writes=(f"cwb{nsl}",))
        for mi in range(MC):
            for wi in range(2):
                par = (mi * 2 + wi) % 2
                rn = ["pA.0", "pA.1"][par]
                w0 = wi * 512
                mms = [(pwin(rn),
                        w1s[sl][:, lc * 1024 + mi * 128:lc * 1024 + (mi + 1) * 128],
                        xlat[:, lc * 1024 + w0:lc * 1024 + w0 + 512],
                        lc == 0, lc == MC - 1) for lc in range(MC)]
                mmg(mms, reads=(f"w1s{sl}", "xlat"), writes=(rn,))
                scpy(ubuf[par][:, :], pwin(rn), reads=(rn,),
                     writes=(f"ubuf{par}",), func=AF.Relu)
                scpy(h2b[par][:, :], ubuf[par][:, :], reads=(f"ubuf{par}",),
                     writes=(f"h2b{par}",), func=AF.Square)
                vop(lambda v, mi=mi, w0=w0, par=par, sl=sl: v.tensor_tensor(
                    hbf[:, mi * 1024 + w0:mi * 1024 + w0 + 512],
                    h2b[par][:, :], cwb[sl][:, w0:w0 + 512], ALU.mult),
                    reads=(f"h2b{par}", f"cwb{sl}"), writes=(f"hbf.{mi}",))
        for ml in range(MC):
            rny = ["pB.0", "pC.0"][ml % 2]
            rny2 = ["pB.1", "pC.1"][ml % 2]
            if ml == 0:
                members = []
                for ic in range(MC):
                    def w2mm(t, ic=ic, ml=ml, rny=rny, rny2=rny2):
                        t.matmul(pwin(rny),
                                 w2s[:, ic * 1024 + ml * 128:ic * 1024 + (ml + 1) * 128],
                                 hbf[:, ic * 1024 + 0:ic * 1024 + 512],
                                 start=(ic == 0), stop=(ic == MC - 1))
                        return t.matmul(pwin(rny2),
                                        w2s[:, ic * 1024 + ml * 128:ic * 1024 + (ml + 1) * 128],
                                        hbf[:, ic * 1024 + 512:ic * 1024 + 1024],
                                        start=(ic == 0), stop=(ic == MC - 1))
                    members.append((w2mm, ("w2s", f"hbf.{ic}")))
                B.member_ops("tensor", members, writes=(rny, rny2))
            else:
                mms = []
                for ic in range(MC):
                    mms.append((pwin(rny),
                                w2s[:, ic * 1024 + ml * 128:ic * 1024 + (ml + 1) * 128],
                                hbf[:, ic * 1024 + 0:ic * 1024 + 512],
                                ic == 0, ic == MC - 1))
                for ic in range(MC):
                    mms.append((pwin(rny2),
                                w2s[:, ic * 1024 + ml * 128:ic * 1024 + (ml + 1) * 128],
                                hbf[:, ic * 1024 + 512:ic * 1024 + 1024],
                                ic == 0, ic == MC - 1))
                mmg(mms, reads=tuple(f"hbf.{i}" for i in range(MC)) + ("w2s",),
                    writes=(rny, rny2))
            pfull = PS[rny[:2]]
            if e == 0:
                vop(lambda v, ml=ml, pfull=pfull: v.tensor_copy(
                    yacc[:, ml * 1024:(ml + 1) * 1024], pfull[:, :]),
                    reads=(rny, rny2), writes=("yacc",))
            else:
                vop(lambda v, ml=ml, pfull=pfull: v.tensor_tensor(
                    yacc[:, ml * 1024:(ml + 1) * 1024],
                    yacc[:, ml * 1024:(ml + 1) * 1024], pfull[:, :], ALU.add),
                    reads=(rny, rny2, "yacc"), writes=("yacc",))
        if e + 1 < NE:
            dma("gpsimd", "s_w2", w2s[:, :], d_w2[e + 1], writes=("w2s",))
    if "yacc" in d_dbg:
        dma("gpsimd", "s_dbg4", d_dbg["yacc"], yacc[:, :], reads=("yacc",))
        B.drain("gpsimd", "s_dbg4")

    B.retire("cwb0", "cwb1", "w1s0", "w1s1", "w2s", "xlat",
             *[f"hbf.{i}" for i in range(MC)])
    B.new_res("ylat", "fc2s0", "fc2s1")
    for ml in range(MC):
        scpy(ylat[:, ml * 1024:(ml + 1) * 1024],
             yacc[:, ml * 1024:(ml + 1) * 1024], reads=("yacc",),
             writes=("ylat",))
    dma("gpsimd", "s_fc20", fc2s[0][:, :], d_fc2[0], writes=("fc2s0",))
    for mo in range(DC):
        sl = mo % 2
        if mo + 1 < DC:
            nsl = (mo + 1) % 2
            dma("gpsimd", f"s_fc2{nsl}", fc2s[nsl][:, :], d_fc2[mo + 1],
                writes=(f"fc2s{nsl}",))
        rny = ["pB.0", "pC.0"][mo % 2]
        rny2 = ["pB.1", "pC.1"][mo % 2]
        mms = []
        for lc in range(MC):
            mms.append((pwin(rny), fc2s[sl][:, lc * 128:(lc + 1) * 128],
                        ylat[:, lc * 1024 + 0:lc * 1024 + 512],
                        lc == 0, lc == MC - 1))
        for lc in range(MC):
            mms.append((pwin(rny2), fc2s[sl][:, lc * 128:(lc + 1) * 128],
                        ylat[:, lc * 1024 + 512:lc * 1024 + 1024],
                        lc == 0, lc == MC - 1))
        mmg(mms, reads=(f"fc2s{sl}", "ylat"), writes=(rny, rny2))
        pfull = PS[rny[:2]]
        vop(lambda v, mo=mo, pfull=pfull: v.tensor_tensor(
            acc[:, mo * 1024:(mo + 1) * 1024],
            acc[:, mo * 1024:(mo + 1) * 1024], pfull[:, :], ALU.add),
            reads=(rny, rny2, f"a{mo}"), writes=(f"a{mo}",))

    B.retire("ylat", "fc2s0", "fc2s1", "yacc")
    B.new_res("rows2", "sqbf20", "sqbf21", "sq2bf20", "sq2bf21",
              "outst0", "outst1", "y")

    def ln2_apply(m):
        osl = m % 2
        vop(lambda v, m=m, osl=osl: v.tensor_tensor(
            outst[osl][:, :], acc[:, m * 1024:(m + 1) * 1024],
            pC[:, 0:1024], ALU.mult),
            reads=("pC.0", "pC.1", f"a{m}"), writes=(f"outst{osl}",))
        vop(lambda v, osl=osl: v.tensor_tensor(
            outst[osl][:, :], outst[osl][:, :], pA[:, 0:1024], ALU.add),
            reads=("pA.0", "pA.1"), writes=(f"outst{osl}",))
        dma("sync", "s_y", d_y[m * 128:(m + 1) * 128, :], outst[osl][:, :],
            reads=(f"outst{osl}",), writes=("y",))

    layernorm(rows2, "rows2", sqbf2, sq2bf2, "sqbf2", "sq2bf2", ln2_apply)
    B.drain("sync", "s_y")

    with nc.Block() as block:
        B.replay(block)
    return nc


def _prep_weights(i):
    f = np.ascontiguousarray
    w = {}
    w["wq"] = f(i["L0_attn_Wq"].reshape(16, 128, 16, 128).transpose(2, 1, 0, 3)
                .reshape(16, 128, 2048))
    w["wk"] = f(i["L0_attn_Wk"].reshape(4, 128, 16, 128).transpose(2, 1, 0, 3)
                .reshape(16, 128, 512))
    w["wv"] = f(i["L0_attn_Wv"].reshape(4, 128, 16, 128).transpose(2, 1, 0, 3)
                .reshape(16, 128, 512))
    w["wkv"] = f(i["L0_attn_Wkv"].reshape(16, 128, 4, 128).transpose(1, 0, 2, 3)
                 .reshape(128, 8192))
    w["wo"] = f(i["L0_attn_Wo"].reshape(16, 128, 16, 128).transpose(2, 1, 0, 3)
                .reshape(16, 128, 2048))
    w["gw"] = f(i["L0_ffn_gate_w"].reshape(32, 16, 128).transpose(2, 1, 0)
                .reshape(128, 512))
    w["biasb"] = f(np.broadcast_to(i["L0_ffn_gate_bias"], (128, 32)))
    w["fc1"] = f(i["L0_ffn_fc1_w"].reshape(8, 128, 16, 128).transpose(0, 3, 2, 1)
                 .reshape(8, 128, 2048))
    w["fc2"] = f(i["L0_ffn_fc2_w"].reshape(16, 128, 8, 128).transpose(0, 3, 2, 1)
                 .reshape(16, 128, 1024))
    w["up"] = f(i["L0_ffn_shared_up_w"].reshape(32, 128, 16, 128)
                .transpose(0, 3, 2, 1).reshape(32, 128, 2048))
    w["dn"] = f(i["L0_ffn_shared_down_w"].reshape(16, 128, 32, 128)
                .transpose(0, 3, 2, 1).reshape(16, 128, 4096))
    w["w1"] = f(i["L0_ffn_w1"].reshape(32, 1024, 8, 128).transpose(0, 3, 2, 1)
                .reshape(32, 128, 8192))
    w["w2"] = f(i["L0_ffn_w2"].reshape(32, 1024, 8, 128).transpose(0, 3, 2, 1)
                .reshape(32, 128, 8192))
    w["ident"] = np.eye(128, dtype=np.float32).astype(ml_dtypes.bfloat16)
    return w


def _masks_for_core(c):
    out = np.zeros((NQT, 128, 272), dtype=np.float32)
    j = np.arange(NQT)[:, None, None]
    a = np.arange(128)[None, :, None]
    qt = 1024 * c + 128 * j + a
    b = np.arange(256)[None, None, :]
    kt = 1024 * c - 128 + 128 * j + b
    ok_main = (kt >= 0) & (kt <= qt) & (((qt - kt) < 128) | (kt < 16))
    out[:, :, 0:256] = np.where(ok_main, 1.0, 0.0)
    bp = np.arange(16)[None, None, :]
    kstart = 1024 * c - 128 + 128 * j
    in_main = (bp >= kstart) & (bp < kstart + 256)
    ok_p = (bp <= qt) & (~in_main)
    out[:, :, 256:272] = np.where(ok_p, 1.0, 0.0)
    return out.astype(ml_dtypes.bfloat16)


def _xkv_for_core(x, c):
    own = x[1024 * c:1024 * (c + 1)]
    if c == 0:
        halo = np.zeros((128, D), dtype=np.float32)
    else:
        halo = x[1024 * c - 128:1024 * c]
    persist = x[0:16]
    blk = np.concatenate([halo, own, persist], axis=0)
    return np.ascontiguousarray(blk.T)


def kernel(**inputs):
    x = np.asarray(inputs["L0_attn_x"], dtype=np.float32).reshape(8192, 2048)
    w = _prep_weights({k: np.asarray(v, dtype=np.float32)
                       for k, v in inputs.items() if k != "L0_attn_x"})
    nc = build_nc()
    in_maps = []
    for c in range(NCORES):
        m = dict(w)
        m["xkv"] = _xkv_for_core(x, c)
        m["mask"] = _masks_for_core(c)
        in_maps.append(m)
    res = run_bass_kernel_spmd(nc, in_maps, core_ids=list(range(NCORES)))
    out = np.empty((8192, 2048), dtype=np.float32)
    for c in range(NCORES):
        out[1024 * c:1024 * (c + 1), :] = res.results[c]["y"].T
    return out.reshape(1, 8192, 2048)


# version 07
# speedup vs baseline: 1.1634x; 1.1634x over previous
import numpy as np
import ml_dtypes
import concourse.bass as bass
import concourse.mybir as mybir
from concourse.bass_utils import run_bass_kernel_spmd

F32 = mybir.dt.float32
BF16 = mybir.dt.bfloat16
AF = mybir.ActivationFunctionType
ALU = mybir.AluOpType
AX = mybir.AxisListType

NCORES = 8
D = 2048
NH = 16
T = 1024
TH = 1168
NE = 32
NQT = 8
SCALE = float(1.0 / np.sqrt(128.0))
BIG = 1.0e30

DC = D // 128
LC = 4
MC = 8
SC = 32


class Bld:
    STREAMS = ("tensor", "vector", "scalar", "sync", "gpsimd")

    def __init__(self, nc):
        self.nc = nc
        self.streams = {s: [] for s in self.STREAMS}
        self.semh = {}
        self.semcnt = {}
        self.known = {s: {} for s in self.STREAMS}
        self.clock_at = {}
        self.res = {}
        self.pending_alias = []

    def sem(self, name):
        if name not in self.semh:
            self.semh[name] = self.nc.alloc_semaphore(name=name)
            self.semcnt[name] = 0
        return self.semh[name]

    def new_res(self, *names):
        for n in names:
            assert n not in self.res, n
            self.res[n] = {"w": None, "r": list(self.pending_alias)}

    def retire(self, *names):
        pend = list(self.pending_alias)
        for n in names:
            r = self.res.pop(n, None)
            if r is None:
                continue
            if r["w"] is not None:
                pend.append(r["w"])
            pend.extend(r["r"])
        best = {}
        for (s, v) in pend:
            best[s] = max(best.get(s, 0), v)
        self.pending_alias = list(best.items())

    def op(self, stream, emit, reads=(), writes=(), sem=None, inc=1):
        if sem is None:
            sem = {"tensor": "pe", "vector": "ve", "scalar": "se"}[stream]
        self.sem(sem)
        need = {}

        def add(m):
            s, v = m
            if need.get(s, 0) < v:
                need[s] = v

        for r in reads:
            rr = self.res[r]
            if rr["w"] is not None:
                add(rr["w"])
        for w in writes:
            rr = self.res[w]
            if rr["w"] is not None:
                add(rr["w"])
            for m in rr["r"]:
                add(m)
        kn = self.known[stream]
        waits = []
        for s, v in sorted(need.items()):
            if kn.get(s, 0) >= v:
                continue
            waits.append((s, v))
            for ks, kv in self.clock_at.get((s, v), {}).items():
                if kn.get(ks, 0) < kv:
                    kn[ks] = kv
            kn[s] = max(kn.get(s, 0), v)
        self.semcnt[sem] += inc
        cnt = self.semcnt[sem]
        if stream == "tensor":
            kn[sem] = cnt
        m = (sem, cnt)
        ca = dict(kn)
        ca[sem] = max(ca.get(sem, 0), cnt)
        self.clock_at[m] = ca
        for r in reads:
            lst = self.res[r]["r"]
            for i, (s0, _) in enumerate(lst):
                if s0 == sem:
                    lst[i] = m
                    break
            else:
                lst.append(m)
        for w in writes:
            self.res[w] = {"w": m, "r": []}
        self.streams[stream].append((waits, emit, sem, inc))
        return m

    def member_ops(self, stream, members, writes=(), sem=None, inc=1):
        if sem is None:
            sem = {"tensor": "pe", "vector": "ve", "scalar": "se"}[stream]
        self.sem(sem)
        kn = self.known[stream]
        all_reads = []
        for i, (emit, reads) in enumerate(members):
            need = {}

            def add(mm):
                s, v = mm
                if need.get(s, 0) < v:
                    need[s] = v

            for r in reads:
                rr = self.res[r]
                if rr["w"] is not None:
                    add(rr["w"])
                all_reads.append(r)
            if i == 0:
                for w in writes:
                    rr = self.res[w]
                    if rr["w"] is not None:
                        add(rr["w"])
                    for mm in rr["r"]:
                        add(mm)
            waits = []
            for s, v in sorted(need.items()):
                if kn.get(s, 0) >= v:
                    continue
                waits.append((s, v))
                for ks, kv in self.clock_at.get((s, v), {}).items():
                    if kn.get(ks, 0) < kv:
                        kn[ks] = kv
                kn[s] = max(kn.get(s, 0), v)
            last = i == len(members) - 1
            self.streams[stream].append(
                (waits, emit, sem if last else None, inc if last else 0))
        self.semcnt[sem] += inc
        cnt = self.semcnt[sem]
        if stream == "tensor":
            kn[sem] = cnt
        m = (sem, cnt)
        ca = dict(kn)
        ca[sem] = max(ca.get(sem, 0), cnt)
        self.clock_at[m] = ca
        for r in all_reads:
            lst = self.res[r]["r"]
            for i, (s0, _) in enumerate(lst):
                if s0 == sem:
                    lst[i] = m
                    break
            else:
                lst.append(m)
        for w in writes:
            self.res[w] = {"w": m, "r": []}
        return m

    def wait_only(self, stream, sems_vals):
        self.streams[stream].append((list(sems_vals), None, None, 0))

    def drain(self, stream, sem):
        self.wait_only(stream, [(sem, self.semcnt[sem])])

    def replay(self, block):
        for sname in self.STREAMS:
            items = self.streams[sname]
            if not items:
                continue

            def make(items=items):
                def body(eng):
                    for waits, emit, sem, inc in items:
                        for (s, v) in waits:
                            eng.wait_ge(self.semh[s], v)
                        if emit is not None:
                            ins = emit(eng)
                            if sem is not None:
                                ins.then_inc(self.semh[sem], inc)
                return body

            getattr(block, sname)(make())


def build_nc(dbg=()):
    nc = bass.Bass("TRN2", target_bir_lowering=False)

    d_xkv = nc.dram_tensor("xkv", [D, TH], F32, kind="ExternalInput").ap()
    d_mask = nc.dram_tensor("mask", [NQT, 128, 272], BF16, kind="ExternalInput").ap()
    d_ident = nc.dram_tensor("ident", [128, 128], BF16, kind="ExternalInput").ap()
    d_biasb = nc.dram_tensor("biasb", [128, 32], F32, kind="ExternalInput").ap()
    d_gw = nc.dram_tensor("gw", [128, 512], F32, kind="ExternalInput").ap()
    d_wq = nc.dram_tensor("wq", [NH, 128, 2048], F32, kind="ExternalInput").ap()
    d_wk = nc.dram_tensor("wk", [NH, 128, 512], F32, kind="ExternalInput").ap()
    d_wv = nc.dram_tensor("wv", [NH, 128, 512], F32, kind="ExternalInput").ap()
    d_wkv = nc.dram_tensor("wkv", [128, 8192], F32, kind="ExternalInput").ap()
    d_wo = nc.dram_tensor("wo", [DC, 128, 2048], F32, kind="ExternalInput").ap()
    d_fc1 = nc.dram_tensor("fc1", [MC, 128, 2048], F32, kind="ExternalInput").ap()
    d_fc2 = nc.dram_tensor("fc2", [DC, 128, 1024], F32, kind="ExternalInput").ap()
    d_up = nc.dram_tensor("up", [SC, 128, 2048], F32, kind="ExternalInput").ap()
    d_dn = nc.dram_tensor("dn", [DC, 128, 4096], F32, kind="ExternalInput").ap()
    d_w1 = nc.dram_tensor("w1", [NE, 128, 8192], F32, kind="ExternalInput").ap()
    d_w2 = nc.dram_tensor("w2", [NE, 128, 8192], F32, kind="ExternalInput").ap()
    d_cwscr = nc.dram_tensor("cwscr", [NE, 1024], BF16, kind="Internal")
    d_y = nc.dram_tensor("y", [D, T], F32, kind="ExternalOutput").ap()
    d_dbg = {}
    for name, shape in dbg:
        d_dbg[name] = nc.dram_tensor("dbg_" + name, list(shape), F32,
                                     kind="ExternalOutput").ap()

    SBBASE = 16512

    def sb(name, shape, dtype, off):
        return nc.alloc_sbuf_tensor_at(name, shape, dtype, offset=SBBASE + off)

    acc = sb("acc", [128, DC * 1024], F32, 0)
    col1 = sb("col1", [128, 32], F32, 65536)
    ones_f = sb("ones_f", [128, 128], F32, 65664)
    ones_b = sb("ones_b", [128, 16], BF16, 66176)
    ident = sb("ident", [128, 128], BF16, 66208)
    gw = sb("gw", [128, 512], F32, 66464)
    biasb = sb("biasb", [128, 32], F32, 68512)
    rt = sb("rt", [128, 256], F32, 68640)
    cwtok = sb("cwtok", [128, 256], F32, 69664)
    masks = sb("masks", [128, NQT * 272], BF16, 70688)
    cwfm = sb("cwfm", [32, 1024], BF16, 75040)
    cwbf = sb("cwbf", [128, 32], BF16, 77088)
    AB = 77152
    AEND = 229376 - 16512
    x0b = sb("x0b", [128, DC * TH], BF16, AB + 0)
    kvb = sb("kvb", [128, LC * TH], BF16, AB + 37376)
    ao = sb("ao", [128, NH * 1024], BF16, AB + 46720)
    qh = [sb("qh0", [128, 1024], BF16, AB + 79488),
          sb("qh1", [128, 1024], BF16, AB + 126464)]
    kh = [sb("kh0", [128, TH], BF16, AB + 81536),
          sb("kh1", [128, TH], BF16, AB + 128512)]
    vh = [sb("vh0", [128, 1280], BF16, AB + 83872),
          sb("vh1", [128, 1280], BF16, AB + 130848)]
    sprob = [sb("sprob0", [128, 272], F32, AB + 86432),
             sb("sprob1", [128, 272], F32, AB + 133408)]
    pbf = [sb("pbf0", [128, 272], BF16, AB + 87520),
           sb("pbf1", [128, 272], BF16, AB + 134496)]
    pTe = sb("pTe", [128, 768], BF16, AB + 88064)
    wkh = [sb("wkh0", [128, 512], BF16, AB + 89600),
           sb("wkh1", [128, 512], BF16, AB + 90624)]
    wvh = [sb("wvh0", [128, 512], BF16, AB + 91648),
           sb("wvh1", [128, 512], BF16, AB + 92672)]
    wqh = [sb("wqh0", [128, 2048], BF16, AB + 93696),
           sb("wqh1", [128, 2048], BF16, AB + 97792)]
    woh = [sb("woh0", [128, 2048], BF16, AB + 101888),
           sb("woh1", [128, 2048], BF16, AB + 105984)]
    wkvb = sb("wkvb", [128, 8192], BF16, AB + 110080)
    assert AB + 135040 <= AEND, AB
    x1b = sb("x1b", [128, DC * 1024], BF16, AB + 0)
    rows = sb("rows", [1, 3072], F32, AB + 32768)
    sqbf = [sb("sqbf0", [128, 1024], BF16, AB + 45056),
            sb("sqbf1", [128, 1024], BF16, AB + 47104)]
    sq2bf = [sb("sq2bf0", [128, 1024], BF16, AB + 49152),
             sb("sq2bf1", [128, 1024], BF16, AB + 51200)]
    xlat = sb("xlat", [128, MC * 1024], BF16, AB + 53248)
    fc1s = [sb("fc1s0", [128, 2048], BF16, AB + 32768),
            sb("fc1s1", [128, 2048], BF16, AB + 36864)]
    ups = [sb("ups0", [128, 2048], BF16, AB + 32768),
           sb("ups1", [128, 2048], BF16, AB + 36864)]
    dns = [sb("dns0", [128, 1024], BF16, AB + 40960),
           sb("dns1", [128, 1024], BF16, AB + 43008)]
    interb = sb("interb", [128, 8 * 1024], BF16, AB + 69632)
    ubuf = [sb("ubuf0", [128, 512], F32, AB + 102400),
            sb("ubuf1", [128, 512], F32, AB + 104448)]
    h2b = [sb("h2b0", [128, 512], BF16, AB + 106496),
           sb("h2b1", [128, 512], BF16, AB + 107520)]
    yacc = sb("yacc", [128, MC * 1024], F32, AB + 0)
    hbf = sb("hbf", [128, MC * 1024], BF16, AB + 36864)
    cwb = [sb("cwb0", [128, 1024], BF16, AB + 108544),
           sb("cwb1", [128, 1024], BF16, AB + 110592)]
    w1s = [sb("w1s0", [128, 8192], BF16, AB + 86016),
           sb("w1s1", [128, 8192], BF16, AB + 69632)]
    w2s = sb("w2s", [128, 8192], BF16, AB + 112640)
    assert AB + 129024 <= AEND
    ylat = sb("ylat", [128, MC * 1024], BF16, AB + 36864)
    fc2s = [sb("fc2s0", [128, 1024], BF16, AB + 108544),
            sb("fc2s1", [128, 1024], BF16, AB + 110592)]
    outst = [sb("outst0", [128, 1024], F32, AB + 53248),
             sb("outst1", [128, 1024], F32, AB + 57344)]
    rows2 = sb("rows2", [1, 3072], F32, AB + 69632)
    sqbf2 = [sb("sqbf20", [128, 1024], BF16, AB + 81920),
             sb("sqbf21", [128, 1024], BF16, AB + 83968)]
    rbc1 = sb("rbc1", [128, 1024], F32, AB + 53248)
    mbc1 = sb("mbc1", [128, 1024], F32, AB + 57344)
    rbc2 = sb("rbc2", [128, 1024], F32, AB + 90112)
    mbc2 = sb("mbc2", [128, 1024], F32, AB + 94208)
    sq2bf2 = [sb("sq2bf20", [128, 1024], BF16, AB + 86016),
              sb("sq2bf21", [128, 1024], BF16, AB + 88064)]

    pA = nc.alloc_psum_tensor("pA", [128, 1024], F32)
    pB = nc.alloc_psum_tensor("pB", [128, 1024], F32)
    pC = nc.alloc_psum_tensor("pC", [128, 1024], F32)
    pT1 = nc.alloc_psum_tensor("pT1", [128, 512], BF16)
    pT2 = nc.alloc_psum_tensor("pT2", [128, 512], BF16)
    PS = {"pA": pA, "pB": pB, "pC": pC}

    def pwin(rn):
        t = PS[rn[:2]]
        half = int(rn[-1])
        return t[:, half * 512:(half + 1) * 512]

    B = Bld(nc)
    B.new_res(*[f"a{i}" for i in range(DC)])
    B.new_res("col1", "ones_f", "ones_b", "ident", "gw", "biasb", "rt",
              "cwtok", "masks", "cwfm", "cwbf",
              "x0b", "kvb", "ao", "qh0", "qh1", "kh0", "kh1", "vh0", "vh1",
              "sprob0", "sprob1", "pbf0", "pbf1", "colp0", "colp1", "pTe",
              "wkh0", "wkh1", "wvh0", "wvh1", "wqh0", "wqh1", "woh0", "woh1",
              "wkvb",
              "pA.0", "pA.1", "pB.0", "pB.1", "pC.0", "pC.1", "pT1", "pT2",
              "cwscr")

    def dma(stream, sem, out_ap, in_ap, reads=(), writes=()):
        def emit(e, o=out_ap, i=in_ap):
            return e.dma_start(o, i)
        return B.op(stream, emit, reads=reads, writes=writes, sem=sem, inc=16)

    def mmg(mms, reads, writes):
        def emit(t, mms=mms):
            ins = None
            for (o, l, r, st, sp) in mms:
                ins = t.matmul(o, l, r, start=st, stop=sp)
            return ins
        return B.op("tensor", emit, reads=reads, writes=writes)

    def scpy(out_ap, in_ap, reads, writes, func=AF.Copy, scale=1.0, bias=0.0):
        def emit(s, o=out_ap, i=in_ap, f=func, sc=scale, b=bias):
            return s.activation(o, i, f, bias=b, scale=sc)
        return B.op("scalar", emit, reads=reads, writes=writes)

    def vop(fn, reads, writes):
        return B.op("vector", fn, reads=reads, writes=writes)

    s_in_h = B.sem("s_in")

    def emit_in(e):
        e.dma_start(acc[:, :].rearrange("p (m t) -> p m t", m=DC),
                    d_xkv[:, 128:1152].rearrange("(m p) t -> p m t", p=128)
                    ).then_inc(s_in_h, 16)
        e.dma_start(masks[:, :].rearrange("p (j k) -> p j k", j=NQT),
                    d_mask.rearrange("j p k -> p j k")).then_inc(s_in_h, 16)
        e.dma_start(ident[:, :], d_ident).then_inc(s_in_h, 16)
        e.dma_start(biasb[:, :], d_biasb).then_inc(s_in_h, 16)
        return e.dma_start(gw[:, :], d_gw)
    B.semcnt["s_in"] += 4 * 16
    B.op("sync", emit_in,
         writes=tuple(f"a{i}" for i in range(DC)) + ("masks", "ident", "biasb", "gw"),
         sem="s_in", inc=16)

    dma("gpsimd", "s_x0b", x0b[:, :].rearrange("p (m t) -> p m t", m=DC),
        d_xkv.rearrange("(m p) t -> p m t", p=128), writes=("x0b",))
    dma("gpsimd", "s_wkv", wkvb[:, :], d_wkv, writes=("wkvb",))
    dma("gpsimd", "s_wq0", wqh[0][:, :], d_wq[0], writes=("wqh0",))
    dma("gpsimd", "s_wk0", wkh[0][:, :], d_wk[0], writes=("wkh0",))
    dma("gpsimd", "s_wv0", wvh[0][:, :], d_wv[0], writes=("wvh0",))

    vop(lambda v: v.memset(ones_f[:, :], 1.0), (), ("ones_f",))
    vop(lambda v: v.memset(ones_b[:, :], 1.0), (), ("ones_b",))
    vop(lambda v: v.memset(col1[:, 6:7], 1e-5), (), ("col1",))
    vop(lambda v: v.memset(col1[:, 7:8], -20.0), (), ("col1",))
    vop(lambda v: v.memset(rt[:, 76:80], -BIG), (), ("rt",))

    KWIN = [(0, 512), (512, 1024), (1024, TH)]
    rot6 = ["pA.0", "pA.1", "pB.0", "pB.1", "pC.0", "pC.1"]
    ri = 0
    for m in range(LC):
        for (w0, w1) in KWIN:
            rn = rot6[ri % 6]
            ri += 1
            mms = [(pwin(rn)[:, 0:w1 - w0],
                    wkvb[:, (fc * LC + m) * 128:(fc * LC + m) * 128 + 128],
                    x0b[:, fc * TH + w0:fc * TH + w1],
                    fc == 0, fc == DC - 1) for fc in range(DC)]
            mmg(mms, reads=("wkvb", "x0b"), writes=(rn,))
            scpy(kvb[:, m * TH + w0:m * TH + w1], pwin(rn)[:, 0:w1 - w0],
                 reads=(rn,), writes=("kvb",))

    def proj_tasks(h):
        sl = h % 2
        tasks = []

        def ktask(wi, w0, w1, sl=sl):
            rn = ["pA.0", "pA.1", "pA.0"][wi]
            mms = [(pwin(rn)[:, 0:w1 - w0],
                    wkh[sl][:, lc * 128:(lc + 1) * 128],
                    kvb[:, lc * TH + w0:lc * TH + w1],
                    lc == 0, lc == LC - 1) for lc in range(LC)]
            mmg(mms, reads=(f"wkh{sl}", "kvb"), writes=(rn,))
            scpy(kh[sl][:, w0:w1], pwin(rn)[:, 0:w1 - w0], reads=(rn,),
                 writes=(f"kh{sl}",))

        for wi, (w0, w1) in enumerate(KWIN):
            tasks.append(lambda wi=wi, w0=w0, w1=w1: ktask(wi, w0, w1))

        def vtask(tt, sl=sl):
            t0 = tt * 128
            tn = min(128, TH - t0)
            rn = ["pA.0", "pA.1"][tt % 2]
            mms = [(pwin(rn)[0:tn, 0:128],
                    kvb[:, lc * TH + t0:lc * TH + t0 + tn],
                    wvh[sl][:, lc * 128:(lc + 1) * 128],
                    lc == 0, lc == LC - 1) for lc in range(LC)]
            mmg(mms, reads=("kvb", f"wvh{sl}"), writes=(rn,))
            scpy(vh[sl][0:tn, t0:t0 + 128], pwin(rn)[0:tn, 0:128], reads=(rn,),
                 writes=(f"vh{sl}",))

        for tt in range(10):
            tasks.append(lambda tt=tt: vtask(tt))

        def qtask(wi, sl=sl):
            rn = ["pA.0", "pA.1"][wi]
            w0 = wi * 512
            mms = [(pwin(rn),
                    wqh[sl][:, fc * 128:(fc + 1) * 128],
                    x0b[:, fc * TH + 128 + w0:fc * TH + 128 + w0 + 512],
                    fc == 0, fc == DC - 1) for fc in range(DC)]
            mmg(mms, reads=(f"wqh{sl}", "x0b"), writes=(rn,))
            scpy(qh[sl][:, w0:w0 + 512], pwin(rn), reads=(rn,),
                 writes=(f"qh{sl}",), scale=SCALE)

        for wi in range(2):
            tasks.append(lambda wi=wi: qtask(wi))
        return tasks

    def emit_scores(h, qt):
        sl = h % 2
        rn = ["pB.0", "pB.1"][qt % 2]
        mms = [(pwin(rn)[:, 0:256], qh[sl][:, qt * 128:(qt + 1) * 128],
                kh[sl][:, qt * 128:qt * 128 + 256], True, True),
               (pwin(rn)[:, 256:272], qh[sl][:, qt * 128:(qt + 1) * 128],
                kh[sl][:, 1152:1168], True, True)]
        mmg(mms, reads=(f"qh{sl}", f"kh{sl}"), writes=(rn,))

    def emit_softmaxA(h, qt):
        p = qt % 2
        rn = ["pB.0", "pB.1"][p]
        sp = sprob[p]
        scpy(sp[:, :], pwin(rn)[:, 0:272], reads=(rn, "col1"),
             writes=(f"sprob{p}",), func=AF.Exp, bias=col1[:, 7:8])

    def emit_softmaxB(h, qt):
        p = qt % 2
        sp, pb_, cr = sprob[p], pbf[p], f"colp{p}"
        c0 = 3 * p
        vop(lambda v, qt=qt, sp=sp, c0=c0: v.scalar_tensor_tensor(
            sp[:, :], sp[:, :], 1.0,
            masks[:, qt * 272:(qt + 1) * 272], ALU.mult, ALU.mult,
            accum_out=col1[:, c0 + 1:c0 + 2]),
            reads=(f"sprob{p}", "masks"), writes=(f"sprob{p}", cr))
        vop(lambda v, c0=c0: v.reciprocal(col1[:, c0 + 2:c0 + 3],
                                          col1[:, c0 + 1:c0 + 2]),
            reads=(cr,), writes=(cr,))
        vop(lambda v, sp=sp, pb_=pb_, c0=c0: v.tensor_single_scalar(
            pb_[:, :], sp[:, :], col1[:, c0 + 2:c0 + 3], ALU.mult),
            reads=(f"sprob{p}", cr), writes=(f"pbf{p}",))

    def emit_avpart(h, qt):
        sl = h % 2
        p = qt % 2
        pb_ = pbf[p]
        ptn = ["pT1", "pT2"][p]
        pt = {"pT1": pT1, "pT2": pT2}[ptn]
        off = 384 * p

        def emt(t, pt=pt, pb_=pb_):
            t.transpose(pt[:, 0:128], pb_[:, 0:128], ident[:, :])
            t.transpose(pt[:, 128:256], pb_[:, 128:256], ident[:, :])
            return t.transpose(pt[0:16, 256:384], pb_[:, 256:272], ident[:, :])
        B.op("tensor", emt, reads=(f"pbf{p}", "ident"), writes=(ptn,))

        def eev(s, pt=pt, off=off):
            s.activation(pTe[:, off:off + 256], pt[:, 0:256], AF.Copy)
            return s.activation(pTe[0:16, off + 256:off + 384],
                                pt[0:16, 256:384], AF.Copy)
        B.op("scalar", eev, reads=(ptn,), writes=("pTe",))

        rn = ["pC.0", "pC.1"][qt % 2]
        mms = [(pwin(rn)[:, 0:128], vh[sl][:, qt * 128:(qt + 1) * 128],
                pTe[:, off:off + 128], True, False),
               (pwin(rn)[:, 0:128], vh[sl][:, (qt + 1) * 128:(qt + 2) * 128],
                pTe[:, off + 128:off + 256], False, False),
               (pwin(rn)[:, 0:128], vh[sl][0:16, 1152:1280],
                pTe[0:16, off + 256:off + 384], False, True)]
        mmg(mms, reads=(f"vh{sl}", "pTe"), writes=(rn,))
        scpy(ao[:, h * 1024 + qt * 128:h * 1024 + (qt + 1) * 128],
             pwin(rn)[:, 0:128], reads=(rn,), writes=("ao",))

    for t_ in proj_tasks(0):
        t_()
    for h in range(NH):
        if h + 1 < NH:
            nsl = (h + 1) % 2
            dma("gpsimd", f"s_wq{nsl}", wqh[nsl][:, :], d_wq[h + 1],
                writes=(f"wqh{nsl}",))
            dma("gpsimd", f"s_wk{nsl}", wkh[nsl][:, :], d_wk[h + 1],
                writes=(f"wkh{nsl}",))
            dma("gpsimd", f"s_wv{nsl}", wvh[nsl][:, :], d_wv[h + 1],
                writes=(f"wvh{nsl}",))
            nxt = proj_tasks(h + 1)
        else:
            nxt = []
        for qt in range(NQT):
            emit_scores(h, qt)
            emit_softmaxA(h, qt)
            if qt > 0:
                emit_softmaxB(h, qt - 1)
                emit_avpart(h, qt - 1)
            for _ in range(2):
                if nxt:
                    nxt.pop(0)()
        emit_softmaxB(h, NQT - 1)
        emit_avpart(h, NQT - 1)
        while nxt:
            nxt.pop(0)()

    dma("gpsimd", "s_wo0", woh[0][:, :], d_wo[0], writes=("woh0",))
    for m in range(DC):
        sl = m % 2
        if m + 1 < DC:
            nsl = (m + 1) % 2
            dma("gpsimd", f"s_wo{nsl}", woh[nsl][:, :], d_wo[m + 1],
                writes=(f"woh{nsl}",))
        for wi in range(2):
            rn = ["pA.0", "pA.1"][wi]
            w0 = wi * 512
            mms = [(pwin(rn),
                    woh[sl][:, c * 128:(c + 1) * 128],
                    ao[:, c * 1024 + w0:c * 1024 + w0 + 512],
                    c == 0, c == DC - 1) for c in range(DC)]
            mmg(mms, reads=(f"woh{sl}", "ao"), writes=(rn,))
            vop(lambda v, m=m, w0=w0, rn=rn: v.tensor_tensor(
                acc[:, m * 1024 + w0:m * 1024 + w0 + 512],
                acc[:, m * 1024 + w0:m * 1024 + w0 + 512], pwin(rn), ALU.add),
                reads=(rn, f"a{m}"), writes=(f"a{m}",))

    if "kvb" in d_dbg:
        dma("gpsimd", "s_dbgk", d_dbg["kvb"], kvb[:, :], reads=("kvb",))
        B.drain("gpsimd", "s_dbgk")
    if "ao" in d_dbg:
        dma("gpsimd", "s_dbga", d_dbg["ao"], ao[:, :], reads=("ao",))
        B.drain("gpsimd", "s_dbga")
    if "pre1" in d_dbg:
        dma("gpsimd", "s_dbgp", d_dbg["pre1"], acc[:, :],
            reads=tuple(f"a{i}" for i in range(DC)))
        B.drain("gpsimd", "s_dbgp")

    B.retire("x0b", "kvb", "ao", "qh", "kh", "vh", "sprob", "pbf", "pTe",
             "wkh0", "wkh1", "wvh0", "wvh1", "wqh0", "wqh1", "woh0", "woh1",
             "wkvb")

    def layernorm(rows_t, rows_rn, sqp, sq2p, sqrn, sq2rn, apply_fn,
                  rbc_t, mbc_t, bcrn):
        statbank = ["pA.0", "pA.1", "pB.0", "pB.1"]
        for m in range(DC):
            sl = m % 2
            if sl == 0:
                scpy(sqp[sl][:, :], acc[:, m * 1024:(m + 1) * 1024],
                     reads=(f"a{m}",), writes=(sqrn + str(sl),))
            else:
                vop(lambda v, m=m, sl=sl: v.tensor_copy(
                    sqp[sl][:, :], acc[:, m * 1024:(m + 1) * 1024]),
                    reads=(f"a{m}",), writes=(sqrn + str(sl),))
            for wi in range(2):
                rn = statbank[wi]
                mmg([(pwin(rn)[0:1, 0:512], ones_b[:, 0:1],
                      sqp[sl][:, wi * 512:wi * 512 + 512], m == 0, m == DC - 1)],
                    reads=(sqrn + str(sl), "ones_b"), writes=(rn,))
            if sl == 0:
                scpy(sq2p[sl][:, :], sqp[sl][:, :],
                     reads=(sqrn + str(sl),), writes=(sq2rn + str(sl),),
                     func=AF.Square)
            else:
                vop(lambda v, sl=sl: v.tensor_mul(
                    sq2p[sl][:, :], sqp[sl][:, :], sqp[sl][:, :]),
                    reads=(sqrn + str(sl),), writes=(sq2rn + str(sl),))
            for wi in range(2):
                rn = statbank[2 + wi]
                mmg([(pwin(rn)[0:1, 0:512], ones_b[:, 0:1],
                      sq2p[sl][:, wi * 512:wi * 512 + 512], m == 0, m == DC - 1)],
                    reads=(sq2rn + str(sl), "ones_b"), writes=(rn,))
        vop(lambda v: v.tensor_copy(rows_t[0:1, 0:1024], pA[0:1, 0:1024]),
            reads=("pA.0", "pA.1"), writes=(rows_rn,))
        vop(lambda v: v.scalar_tensor_tensor(
            rows_t[0:1, 2048:3072], rows_t[0:1, 0:1024], 1.0 / (D * D),
            rows_t[0:1, 0:1024], ALU.mult, ALU.mult),
            reads=(rows_rn,), writes=(rows_rn,))
        vop(lambda v: v.scalar_tensor_tensor(
            rows_t[0:1, 1024:2048], pB[0:1, 0:1024], 1.0 / D,
            rows_t[0:1, 2048:3072], ALU.mult, ALU.subtract),
            reads=("pB.0", "pB.1", rows_rn), writes=(rows_rn,))
        scpy(rows_t[0:1, 1024:2048], rows_t[0:1, 1024:2048],
             reads=(rows_rn, "col1"), writes=(rows_rn,), func=AF.Ln,
             bias=col1[0:1, 6:7])
        scpy(rows_t[0:1, 1024:2048], rows_t[0:1, 1024:2048], reads=(rows_rn,),
             writes=(rows_rn,), func=AF.Exp, scale=-0.5)
        vop(lambda v: v.scalar_tensor_tensor(
            rows_t[0:1, 2048:3072], rows_t[0:1, 0:1024], -1.0 / D,
            rows_t[0:1, 1024:2048], ALU.mult, ALU.mult),
            reads=(rows_rn,), writes=(rows_rn,))
        for wi in range(2):
            rb = ["pC.0", "pC.1"][wi]
            mb = ["pA.0", "pA.1"][wi]
            mmg([(pwin(rb), ones_f[0:1, 0:128],
                  rows_t[0:1, 1024 + wi * 512:1024 + wi * 512 + 512], True, True)],
                reads=(rows_rn, "ones_f"), writes=(rb,))
            mmg([(pwin(mb), ones_f[0:1, 0:128],
                  rows_t[0:1, 2048 + wi * 512:2048 + wi * 512 + 512], True, True)],
                reads=(rows_rn, "ones_f"), writes=(mb,))
        vop(lambda v: v.tensor_copy(rbc_t[:, :], pC[:, 0:1024]),
            reads=("pC.0", "pC.1"), writes=(bcrn + "r",))
        vop(lambda v: v.tensor_copy(mbc_t[:, :], pA[:, 0:1024]),
            reads=("pA.0", "pA.1"), writes=(bcrn + "m",))
        for m in range(DC):
            apply_fn(m)

    B.new_res("rows", "sqbf0", "sqbf1", "sq2bf0", "sq2bf1",
              *[f"xb{i}" for i in range(DC)])

    def ln1_apply(m):
        eng = "vector" if m < 11 else "gpsimd"
        B.op(eng, lambda e, m=m: e.tensor_tensor(
            acc[:, m * 1024:(m + 1) * 1024],
            acc[:, m * 1024:(m + 1) * 1024], rbc1[:, :], ALU.mult),
            reads=("b1r", f"a{m}"), writes=(f"a{m}",),
            sem="ve" if eng == "vector" else "gp")
        B.op(eng, lambda e, m=m: e.tensor_tensor(
            acc[:, m * 1024:(m + 1) * 1024],
            acc[:, m * 1024:(m + 1) * 1024], mbc1[:, :], ALU.add),
            reads=("b1m", f"a{m}"), writes=(f"a{m}",),
            sem="ve" if eng == "vector" else "gp")
        scpy(x1b[:, m * 1024:(m + 1) * 1024], acc[:, m * 1024:(m + 1) * 1024],
             reads=(f"a{m}",), writes=(f"xb{m}",))

    B.new_res("b1r", "b1m")
    layernorm(rows, "rows", sqbf, sq2bf, "sqbf", "sq2bf", ln1_apply, rbc1, mbc1, "b1")
    B.retire("b1r", "b1m")
    B.retire("rows", "sqbf0", "sqbf1", "sq2bf0", "sq2bf1")

    if "x1" in d_dbg:
        dma("gpsimd", "s_dbg1", d_dbg["x1"], acc[:, :],
            reads=tuple(f"a{i}" for i in range(DC)))
        B.drain("gpsimd", "s_dbg1")

    for tt in range(NQT):
        grn = ["pA.0", "pB.0"][tt % 2]
        members = []
        for fc in range(DC):
            def gmm(t, fc=fc, grn=grn, tt=tt):
                return t.matmul(
                    pwin(grn)[:, 0:32],
                    acc[:, fc * 1024 + tt * 128:fc * 1024 + (tt + 1) * 128],
                    gw[:, fc * 32:(fc + 1) * 32],
                    start=(fc == 0), stop=(fc == DC - 1))
            members.append((gmm, (f"a{fc}", "gw")))
        B.member_ops("tensor", members, writes=(grn,))
        scpy(rt[:, 200:232], pwin(grn)[:, 0:32], reads=(grn,), writes=("rt",),
             func=AF.Exp, scale=-1.0)
        vop(lambda v: v.tensor_scalar_add(rt[:, 200:232], rt[:, 200:232], 1.0),
            reads=("rt",), writes=("rt",))
        vop(lambda v: v.reciprocal(rt[:, 0:32], rt[:, 200:232]),
            reads=("rt",), writes=("rt",))
        vop(lambda v: v.tensor_add(rt[:, 32:64], rt[:, 0:32], biasb[:, :]),
            reads=("rt", "biasb"), writes=("rt",))
        for g in range(4):
            vop(lambda v, g=g: v.max(rt[:, 64:72], rt[:, 32 + 8 * g:40 + 8 * g]),
                reads=("rt",), writes=("rt",))
            vop(lambda v, g=g: v.tensor_add(rt[:, 72 + g:73 + g], rt[:, 64:65],
                                            rt[:, 65:66]),
                reads=("rt",), writes=("rt",))
        vop(lambda v: v.max(rt[:, 80:88], rt[:, 72:80]), reads=("rt",), writes=("rt",))
        vop(lambda v: v.tensor_single_scalar(rt[:, 88:92], rt[:, 72:76],
                                             rt[:, 81:82], ALU.is_ge),
            reads=("rt",), writes=("rt",))
        vop(lambda v: v.tensor_scalar(rt[:, 92:96], rt[:, 88:92], 1.0, BIG,
                                      ALU.subtract, ALU.mult),
            reads=("rt",), writes=("rt",))
        for g in range(4):
            vop(lambda v, g=g: v.tensor_single_scalar(
                rt[:, 96 + 8 * g:104 + 8 * g], rt[:, 32 + 8 * g:40 + 8 * g],
                rt[:, 92 + g:93 + g], ALU.add),
                reads=("rt",), writes=("rt",))
        vop(lambda v: v.max(rt[:, 128:136], rt[:, 96:128]), reads=("rt",),
            writes=("rt",))
        vop(lambda v: v.tensor_single_scalar(rt[:, 136:168], rt[:, 96:128],
                                             rt[:, 135:136], ALU.is_ge),
            reads=("rt",), writes=("rt",))
        vop(lambda v: v.tensor_mul(rt[:, 168:200], rt[:, 0:32], rt[:, 136:168]),
            reads=("rt",), writes=("rt",))
        vop(lambda v: v.reduce_sum(col1[:, 8:9], rt[:, 168:200], axis=AX.X),
            reads=("rt",), writes=("col1",))
        vop(lambda v: v.reciprocal(col1[:, 9:10], col1[:, 8:9]),
            reads=("col1",), writes=("col1",))
        vop(lambda v, tt=tt: v.tensor_scalar(cwtok[:, 32 * tt:32 * (tt + 1)],
                                             rt[:, 168:200], col1[:, 9:10], 2.5,
                                             ALU.mult, ALU.mult),
            reads=("rt", "col1"), writes=("cwtok",))
    if "cwtok" in d_dbg:
        dma("gpsimd", "s_dbg2", d_dbg["cwtok"], cwtok[:, :], reads=("cwtok",))
        B.drain("gpsimd", "s_dbg2")

    B.new_res("xlat", "fc1s0", "fc1s1")
    dma("gpsimd", "s_fc10", fc1s[0][:, :], d_fc1[0], writes=("fc1s0",))
    for m in range(MC):
        sl = m % 2
        if m + 1 < MC:
            nsl = (m + 1) % 2
            dma("gpsimd", f"s_fc1{nsl}", fc1s[nsl][:, :], d_fc1[m + 1],
                writes=(f"fc1s{nsl}",))
        for wi in range(2):
            rn = ["pA.0", "pA.1"][wi]
            w0 = wi * 512
            members = []
            for c in range(DC):
                def fmm(t, c=c, w0=w0, sl=sl, rn=rn):
                    return t.matmul(pwin(rn),
                                    fc1s[sl][:, c * 128:(c + 1) * 128],
                                    x1b[:, c * 1024 + w0:c * 1024 + w0 + 512],
                                    start=(c == 0), stop=(c == DC - 1))
                members.append((fmm, (f"fc1s{sl}", f"xb{c}")))
            B.member_ops("tensor", members, writes=(rn,))
            scpy(xlat[:, m * 1024 + w0:m * 1024 + w0 + 512], pwin(rn),
                 reads=(rn,), writes=("xlat",))
    for tt in range(NQT):
        ptn = ["pT1", "pT2"][tt % 2]
        pt = {"pT1": pT1, "pT2": pT2}[ptn]
        vop(lambda v, tt=tt: v.tensor_copy(cwbf[:, :],
                                           cwtok[:, 32 * tt:32 * (tt + 1)]),
            reads=("cwtok",), writes=("cwbf",))

        def emt(t, pt=pt):
            return t.transpose(pt[0:32, 0:128], cwbf[:, :], ident[:, :])
        B.op("tensor", emt, reads=("cwbf", "ident"), writes=(ptn,))
        vop(lambda v, pt=pt, tt=tt: v.tensor_copy(
            cwfm[:, tt * 128:(tt + 1) * 128], pt[0:32, 0:128]),
            reads=(ptn,), writes=("cwfm",))
    dma("sync", "s_cwo", d_cwscr.ap(), cwfm[:, :], reads=("cwfm",),
        writes=("cwscr",))

    if "xlat" in d_dbg:
        dma("gpsimd", "s_dbg3", d_dbg["xlat"], xlat[:, :], reads=("xlat",))
        B.drain("gpsimd", "s_dbg3")

    B.retire("fc1s0", "fc1s1")
    B.new_res("ups0", "ups1", "dns0", "dns1", "ubuf0", "ubuf1",
              "h2b0", "h2b1", "w1s0", "w2s", "cwb0", "cwb1",
              *[f"ib{i}" for i in range(8)])
    for S4 in range(4):
        dma("gpsimd", "s_ups0", ups[0][:, :], d_up[8 * S4], writes=("ups0",))
        for mi in range(8):
            gi = 8 * S4 + mi
            sl = mi % 2
            if mi + 1 < 8:
                nsl = (mi + 1) % 2
                dma("gpsimd", f"s_ups{nsl}", ups[nsl][:, :], d_up[gi + 1],
                    writes=(f"ups{nsl}",))
            for wi in range(2):
                rn = ["pA.0", "pA.1"][wi]
                w0 = wi * 512
                mms = [(pwin(rn),
                        ups[sl][:, c * 128:(c + 1) * 128],
                        x1b[:, c * 1024 + w0:c * 1024 + w0 + 512],
                        c == 0, c == DC - 1) for c in range(DC)]
                mmg(mms, reads=(f"ups{sl}",) + tuple(f"xb{i}" for i in range(DC)),
                    writes=(rn,))
                usl = (mi * 2 + wi) % 2
                scpy(ubuf[usl][:, :], pwin(rn), reads=(rn,),
                     writes=(f"ubuf{usl}",), func=AF.Relu)
                scpy(interb[:, mi * 1024 + w0:mi * 1024 + w0 + 512],
                     ubuf[usl][:, :], reads=(f"ubuf{usl}",),
                     writes=(f"ib{mi}",), func=AF.Square)
        dma("gpsimd", "s_dns0", dns[0][:, :],
            d_dn[0][:, 1024 * S4:1024 * S4 + 1024], writes=("dns0",))
        for mo in range(DC):
            sl = mo % 2
            if mo + 1 < DC:
                nsl = (mo + 1) % 2
                dma("gpsimd", f"s_dns{nsl}", dns[nsl][:, :],
                    d_dn[mo + 1][:, 1024 * S4:1024 * S4 + 1024],
                    writes=(f"dns{nsl}",))
            rny = ["pB.0", "pC.0"][mo % 2]
            rny2 = ["pB.1", "pC.1"][mo % 2]
            members = []
            for cl in range(8):
                def dmm(t, cl=cl, sl=sl, rny=rny, rny2=rny2):
                    t.matmul(pwin(rny), dns[sl][:, cl * 128:(cl + 1) * 128],
                             interb[:, cl * 1024 + 0:cl * 1024 + 512],
                             start=(cl == 0), stop=(cl == 7))
                    return t.matmul(pwin(rny2),
                                    dns[sl][:, cl * 128:(cl + 1) * 128],
                                    interb[:, cl * 1024 + 512:cl * 1024 + 1024],
                                    start=(cl == 0), stop=(cl == 7))
                members.append((dmm, (f"dns{sl}", f"ib{cl}")))
            B.member_ops("tensor", members, writes=(rny, rny2))
            pfull = PS[rny[:2]]
            vop(lambda v, mo=mo, pfull=pfull: v.tensor_tensor(
                acc[:, mo * 1024:(mo + 1) * 1024],
                acc[:, mo * 1024:(mo + 1) * 1024], pfull[:, :], ALU.add),
                reads=(rny, rny2, f"a{mo}"), writes=(f"a{mo}",))
        if S4 == 1:
            dma("gpsimd", "s_w10", w1s[0][:, :], d_w1[0], writes=("w1s0",))
            dma("sync", "s_cwb0", cwb[0][:, :],
                bass.AP(d_cwscr, 0, [[0, 128], [1, 1024]]), reads=("cwscr",),
                writes=("cwb0",))
        elif S4 == 2:
            dma("gpsimd", "s_w2", w2s[:, :], d_w2[0], writes=("w2s",))

    B.retire("ups0", "ups1", "dns0", "dns1",
             *[f"ib{i}" for i in range(8)], *[f"xb{i}" for i in range(DC)])

    B.new_res("yacc", "w1s1", *[f"hbf.{i}" for i in range(MC)])
    for e in range(NE):
        sl = e % 2
        if e + 1 < NE:
            nsl = (e + 1) % 2
            dma("gpsimd", f"s_w1{nsl}", w1s[nsl][:, :], d_w1[e + 1],
                writes=(f"w1s{nsl}",))
            dma("sync", f"s_cwb{nsl}", cwb[nsl][:, :],
                bass.AP(d_cwscr, (e + 1) * 1024, [[0, 128], [1, 1024]]),
                reads=("cwscr",), writes=(f"cwb{nsl}",))
        for mi in range(MC):
            for wi in range(2):
                par = (mi * 2 + wi) % 2
                rn = ["pA.0", "pA.1"][par]
                w0 = wi * 512
                mms = [(pwin(rn),
                        w1s[sl][:, lc * 1024 + mi * 128:lc * 1024 + (mi + 1) * 128],
                        xlat[:, lc * 1024 + w0:lc * 1024 + w0 + 512],
                        lc == 0, lc == MC - 1) for lc in range(MC)]
                mmg(mms, reads=(f"w1s{sl}", "xlat"), writes=(rn,))
                scpy(ubuf[par][:, :], pwin(rn), reads=(rn,),
                     writes=(f"ubuf{par}",), func=AF.Relu)
                scpy(h2b[par][:, :], ubuf[par][:, :], reads=(f"ubuf{par}",),
                     writes=(f"h2b{par}",), func=AF.Square)
                vop(lambda v, mi=mi, w0=w0, par=par, sl=sl: v.tensor_tensor(
                    hbf[:, mi * 1024 + w0:mi * 1024 + w0 + 512],
                    h2b[par][:, :], cwb[sl][:, w0:w0 + 512], ALU.mult),
                    reads=(f"h2b{par}", f"cwb{sl}"), writes=(f"hbf.{mi}",))
        for ml in range(MC):
            rny = ["pB.0", "pC.0"][ml % 2]
            rny2 = ["pB.1", "pC.1"][ml % 2]
            if ml == 0:
                members = []
                for ic in range(MC):
                    def w2mm(t, ic=ic, ml=ml, rny=rny, rny2=rny2):
                        t.matmul(pwin(rny),
                                 w2s[:, ic * 1024 + ml * 128:ic * 1024 + (ml + 1) * 128],
                                 hbf[:, ic * 1024 + 0:ic * 1024 + 512],
                                 start=(ic == 0), stop=(ic == MC - 1))
                        return t.matmul(pwin(rny2),
                                        w2s[:, ic * 1024 + ml * 128:ic * 1024 + (ml + 1) * 128],
                                        hbf[:, ic * 1024 + 512:ic * 1024 + 1024],
                                        start=(ic == 0), stop=(ic == MC - 1))
                    members.append((w2mm, ("w2s", f"hbf.{ic}")))
                B.member_ops("tensor", members, writes=(rny, rny2))
            else:
                mms = []
                for ic in range(MC):
                    mms.append((pwin(rny),
                                w2s[:, ic * 1024 + ml * 128:ic * 1024 + (ml + 1) * 128],
                                hbf[:, ic * 1024 + 0:ic * 1024 + 512],
                                ic == 0, ic == MC - 1))
                for ic in range(MC):
                    mms.append((pwin(rny2),
                                w2s[:, ic * 1024 + ml * 128:ic * 1024 + (ml + 1) * 128],
                                hbf[:, ic * 1024 + 512:ic * 1024 + 1024],
                                ic == 0, ic == MC - 1))
                mmg(mms, reads=tuple(f"hbf.{i}" for i in range(MC)) + ("w2s",),
                    writes=(rny, rny2))
            pfull = PS[rny[:2]]
            if e == 0:
                vop(lambda v, ml=ml, pfull=pfull: v.tensor_copy(
                    yacc[:, ml * 1024:(ml + 1) * 1024], pfull[:, :]),
                    reads=(rny, rny2), writes=("yacc",))
            else:
                vop(lambda v, ml=ml, pfull=pfull: v.tensor_tensor(
                    yacc[:, ml * 1024:(ml + 1) * 1024],
                    yacc[:, ml * 1024:(ml + 1) * 1024], pfull[:, :], ALU.add),
                    reads=(rny, rny2, "yacc"), writes=("yacc",))
        if e + 1 < NE:
            dma("gpsimd", "s_w2", w2s[:, :], d_w2[e + 1], writes=("w2s",))
    if "yacc" in d_dbg:
        dma("gpsimd", "s_dbg4", d_dbg["yacc"], yacc[:, :], reads=("yacc",))
        B.drain("gpsimd", "s_dbg4")

    B.retire("cwb0", "cwb1", "w1s0", "w1s1", "w2s", "xlat",
             *[f"hbf.{i}" for i in range(MC)])
    B.new_res("fc2s0", "fc2s1", *[f"yl{i}" for i in range(MC)])
    for ml in range(MC):
        scpy(ylat[:, ml * 1024:(ml + 1) * 1024],
             yacc[:, ml * 1024:(ml + 1) * 1024], reads=("yacc",),
             writes=(f"yl{ml}",))
    dma("gpsimd", "s_fc20", fc2s[0][:, :], d_fc2[0], writes=("fc2s0",))
    for mo in range(DC):
        sl = mo % 2
        if mo + 1 < DC:
            nsl = (mo + 1) % 2
            dma("gpsimd", f"s_fc2{nsl}", fc2s[nsl][:, :], d_fc2[mo + 1],
                writes=(f"fc2s{nsl}",))
        rny = ["pB.0", "pC.0"][mo % 2]
        rny2 = ["pB.1", "pC.1"][mo % 2]
        members = []
        for lc in range(MC):
            def f2mm(t, lc=lc, sl=sl, rny=rny, rny2=rny2):
                t.matmul(pwin(rny), fc2s[sl][:, lc * 128:(lc + 1) * 128],
                         ylat[:, lc * 1024 + 0:lc * 1024 + 512],
                         start=(lc == 0), stop=(lc == MC - 1))
                return t.matmul(pwin(rny2), fc2s[sl][:, lc * 128:(lc + 1) * 128],
                                ylat[:, lc * 1024 + 512:lc * 1024 + 1024],
                                start=(lc == 0), stop=(lc == MC - 1))
            members.append((f2mm, (f"fc2s{sl}", f"yl{lc}")))
        B.member_ops("tensor", members, writes=(rny, rny2))
        pfull = PS[rny[:2]]
        vop(lambda v, mo=mo, pfull=pfull: v.tensor_tensor(
            acc[:, mo * 1024:(mo + 1) * 1024],
            acc[:, mo * 1024:(mo + 1) * 1024], pfull[:, :], ALU.add),
            reads=(rny, rny2, f"a{mo}"), writes=(f"a{mo}",))

    B.retire("fc2s0", "fc2s1", "yacc", *[f"yl{i}" for i in range(MC)])
    B.new_res("rows2", "sqbf20", "sqbf21", "sq2bf20", "sq2bf21",
              "outst0", "outst1", "y")

    def ln2_apply(m):
        osl = m % 2
        eng = "gpsimd" if (m % 4 == 3) else "vector"
        sm = "gp" if eng == "gpsimd" else "ve"
        B.op(eng, lambda e, m=m, osl=osl: e.tensor_tensor(
            outst[osl][:, :], acc[:, m * 1024:(m + 1) * 1024],
            rbc2[:, :], ALU.mult),
            reads=("b2r", f"a{m}"), writes=(f"outst{osl}",), sem=sm)
        B.op(eng, lambda e, osl=osl: e.tensor_tensor(
            outst[osl][:, :], outst[osl][:, :], mbc2[:, :], ALU.add),
            reads=("b2m",), writes=(f"outst{osl}",), sem=sm)
        dma("sync", "s_y", d_y[m * 128:(m + 1) * 128, :], outst[osl][:, :],
            reads=(f"outst{osl}",), writes=("y",))

    B.new_res("b2r", "b2m")
    layernorm(rows2, "rows2", sqbf2, sq2bf2, "sqbf2", "sq2bf2", ln2_apply,
              rbc2, mbc2, "b2")
    B.drain("sync", "s_y")

    with nc.Block() as block:
        B.replay(block)
    return nc


def _prep_weights(i):
    f = np.ascontiguousarray
    w = {}
    w["wq"] = f(i["L0_attn_Wq"].reshape(16, 128, 16, 128).transpose(2, 1, 0, 3)
                .reshape(16, 128, 2048))
    w["wk"] = f(i["L0_attn_Wk"].reshape(4, 128, 16, 128).transpose(2, 1, 0, 3)
                .reshape(16, 128, 512))
    w["wv"] = f(i["L0_attn_Wv"].reshape(4, 128, 16, 128).transpose(2, 1, 0, 3)
                .reshape(16, 128, 512))
    w["wkv"] = f(i["L0_attn_Wkv"].reshape(16, 128, 4, 128).transpose(1, 0, 2, 3)
                 .reshape(128, 8192))
    w["wo"] = f(i["L0_attn_Wo"].reshape(16, 128, 16, 128).transpose(2, 1, 0, 3)
                .reshape(16, 128, 2048))
    w["gw"] = f(i["L0_ffn_gate_w"].reshape(32, 16, 128).transpose(2, 1, 0)
                .reshape(128, 512))
    w["biasb"] = f(np.broadcast_to(i["L0_ffn_gate_bias"], (128, 32)))
    w["fc1"] = f(i["L0_ffn_fc1_w"].reshape(8, 128, 16, 128).transpose(0, 3, 2, 1)
                 .reshape(8, 128, 2048))
    w["fc2"] = f(i["L0_ffn_fc2_w"].reshape(16, 128, 8, 128).transpose(0, 3, 2, 1)
                 .reshape(16, 128, 1024))
    w["up"] = f(i["L0_ffn_shared_up_w"].reshape(32, 128, 16, 128)
                .transpose(0, 3, 2, 1).reshape(32, 128, 2048))
    w["dn"] = f(i["L0_ffn_shared_down_w"].reshape(16, 128, 32, 128)
                .transpose(0, 3, 2, 1).reshape(16, 128, 4096))
    w["w1"] = f(i["L0_ffn_w1"].reshape(32, 1024, 8, 128).transpose(0, 3, 2, 1)
                .reshape(32, 128, 8192))
    w["w2"] = f(i["L0_ffn_w2"].reshape(32, 1024, 8, 128).transpose(0, 3, 2, 1)
                .reshape(32, 128, 8192))
    w["ident"] = np.eye(128, dtype=np.float32).astype(ml_dtypes.bfloat16)
    return w


def _masks_for_core(c):
    out = np.zeros((NQT, 128, 272), dtype=np.float32)
    j = np.arange(NQT)[:, None, None]
    a = np.arange(128)[None, :, None]
    qt = 1024 * c + 128 * j + a
    b = np.arange(256)[None, None, :]
    kt = 1024 * c - 128 + 128 * j + b
    ok_main = (kt >= 0) & (kt <= qt) & (((qt - kt) < 128) | (kt < 16))
    out[:, :, 0:256] = np.where(ok_main, 1.0, 0.0)
    bp = np.arange(16)[None, None, :]
    kstart = 1024 * c - 128 + 128 * j
    in_main = (bp >= kstart) & (bp < kstart + 256)
    ok_p = (bp <= qt) & (~in_main)
    out[:, :, 256:272] = np.where(ok_p, 1.0, 0.0)
    return out.astype(ml_dtypes.bfloat16)


def _xkv_for_core(x, c):
    own = x[1024 * c:1024 * (c + 1)]
    if c == 0:
        halo = np.zeros((128, D), dtype=np.float32)
    else:
        halo = x[1024 * c - 128:1024 * c]
    persist = x[0:16]
    blk = np.concatenate([halo, own, persist], axis=0)
    return np.ascontiguousarray(blk.T)


def kernel(**inputs):
    x = np.asarray(inputs["L0_attn_x"], dtype=np.float32).reshape(8192, 2048)
    w = _prep_weights({k: np.asarray(v, dtype=np.float32)
                       for k, v in inputs.items() if k != "L0_attn_x"})
    nc = build_nc()
    in_maps = []
    for c in range(NCORES):
        m = dict(w)
        m["xkv"] = _xkv_for_core(x, c)
        m["mask"] = _masks_for_core(c)
        in_maps.append(m)
    res = run_bass_kernel_spmd(nc, in_maps, core_ids=list(range(NCORES)))
    out = np.empty((8192, 2048), dtype=np.float32)
    for c in range(NCORES):
        out[1024 * c:1024 * (c + 1), :] = res.results[c]["y"].T
    return out.reshape(1, 8192, 2048)


# version 08
# speedup vs baseline: 1.2098x; 1.0398x over previous
import numpy as np
import ml_dtypes
import concourse.bass as bass
import concourse.mybir as mybir
from concourse.bass_utils import run_bass_kernel_spmd

F32 = mybir.dt.float32
BF16 = mybir.dt.bfloat16
AF = mybir.ActivationFunctionType
ALU = mybir.AluOpType
AX = mybir.AxisListType

NCORES = 8
D = 2048
NH = 16
T = 1024
TH = 1168
NE = 32
NQT = 8
SCALE = float(1.0 / np.sqrt(128.0))
BIG = 1.0e30

DC = D // 128
LC = 4
MC = 8
SC = 32


class Bld:
    STREAMS = ("tensor", "vector", "scalar", "sync", "gpsimd")

    def __init__(self, nc):
        self.nc = nc
        self.streams = {s: [] for s in self.STREAMS}
        self.semh = {}
        self.semcnt = {}
        self.known = {s: {} for s in self.STREAMS}
        self.clock_at = {}
        self.res = {}
        self.pending_alias = []

    def sem(self, name):
        if name not in self.semh:
            self.semh[name] = self.nc.alloc_semaphore(name=name)
            self.semcnt[name] = 0
        return self.semh[name]

    def new_res(self, *names):
        for n in names:
            assert n not in self.res, n
            self.res[n] = {"w": None, "r": list(self.pending_alias)}

    def retire(self, *names):
        pend = list(self.pending_alias)
        for n in names:
            r = self.res.pop(n, None)
            if r is None:
                continue
            if r["w"] is not None:
                pend.append(r["w"])
            pend.extend(r["r"])
        best = {}
        for (s, v) in pend:
            best[s] = max(best.get(s, 0), v)
        self.pending_alias = list(best.items())

    def op(self, stream, emit, reads=(), writes=(), sem=None, inc=1):
        if sem is None:
            sem = {"tensor": "pe", "vector": "ve", "scalar": "se"}[stream]
        self.sem(sem)
        need = {}

        def add(m):
            s, v = m
            if need.get(s, 0) < v:
                need[s] = v

        for r in reads:
            rr = self.res[r]
            if rr["w"] is not None:
                add(rr["w"])
        for w in writes:
            rr = self.res[w]
            if rr["w"] is not None:
                add(rr["w"])
            for m in rr["r"]:
                add(m)
        kn = self.known[stream]
        waits = []
        for s, v in sorted(need.items()):
            if kn.get(s, 0) >= v:
                continue
            waits.append((s, v))
            for ks, kv in self.clock_at.get((s, v), {}).items():
                if kn.get(ks, 0) < kv:
                    kn[ks] = kv
            kn[s] = max(kn.get(s, 0), v)
        self.semcnt[sem] += inc
        cnt = self.semcnt[sem]
        if stream == "tensor":
            kn[sem] = cnt
        m = (sem, cnt)
        ca = dict(kn)
        ca[sem] = max(ca.get(sem, 0), cnt)
        self.clock_at[m] = ca
        for r in reads:
            lst = self.res[r]["r"]
            for i, (s0, _) in enumerate(lst):
                if s0 == sem:
                    lst[i] = m
                    break
            else:
                lst.append(m)
        for w in writes:
            self.res[w] = {"w": m, "r": []}
        self.streams[stream].append((waits, emit, sem, inc))
        return m

    def member_ops(self, stream, members, writes=(), sem=None, inc=1):
        if sem is None:
            sem = {"tensor": "pe", "vector": "ve", "scalar": "se"}[stream]
        self.sem(sem)
        kn = self.known[stream]
        all_reads = []
        for i, (emit, reads) in enumerate(members):
            need = {}

            def add(mm):
                s, v = mm
                if need.get(s, 0) < v:
                    need[s] = v

            for r in reads:
                rr = self.res[r]
                if rr["w"] is not None:
                    add(rr["w"])
                all_reads.append(r)
            if i == 0:
                for w in writes:
                    rr = self.res[w]
                    if rr["w"] is not None:
                        add(rr["w"])
                    for mm in rr["r"]:
                        add(mm)
            waits = []
            for s, v in sorted(need.items()):
                if kn.get(s, 0) >= v:
                    continue
                waits.append((s, v))
                for ks, kv in self.clock_at.get((s, v), {}).items():
                    if kn.get(ks, 0) < kv:
                        kn[ks] = kv
                kn[s] = max(kn.get(s, 0), v)
            last = i == len(members) - 1
            self.streams[stream].append(
                (waits, emit, sem if last else None, inc if last else 0))
        self.semcnt[sem] += inc
        cnt = self.semcnt[sem]
        if stream == "tensor":
            kn[sem] = cnt
        m = (sem, cnt)
        ca = dict(kn)
        ca[sem] = max(ca.get(sem, 0), cnt)
        self.clock_at[m] = ca
        for r in all_reads:
            lst = self.res[r]["r"]
            for i, (s0, _) in enumerate(lst):
                if s0 == sem:
                    lst[i] = m
                    break
            else:
                lst.append(m)
        for w in writes:
            self.res[w] = {"w": m, "r": []}
        return m

    def wait_only(self, stream, sems_vals):
        self.streams[stream].append((list(sems_vals), None, None, 0))

    def drain(self, stream, sem):
        self.wait_only(stream, [(sem, self.semcnt[sem])])

    def replay(self, block):
        for sname in self.STREAMS:
            items = self.streams[sname]
            if not items:
                continue

            def make(items=items):
                def body(eng):
                    for waits, emit, sem, inc in items:
                        for (s, v) in waits:
                            eng.wait_ge(self.semh[s], v)
                        if emit is not None:
                            ins = emit(eng)
                            if sem is not None:
                                ins.then_inc(self.semh[sem], inc)
                return body

            getattr(block, sname)(make())


def build_nc(dbg=()):
    nc = bass.Bass("TRN2", target_bir_lowering=False)

    d_xkv = nc.dram_tensor("xkv", [D, TH], F32, kind="ExternalInput").ap()
    d_mask = nc.dram_tensor("mask", [NQT, 128, 272], BF16, kind="ExternalInput").ap()
    d_ident = nc.dram_tensor("ident", [128, 128], BF16, kind="ExternalInput").ap()
    d_biasb = nc.dram_tensor("biasb", [128, 32], F32, kind="ExternalInput").ap()
    d_gw = nc.dram_tensor("gw", [128, 512], F32, kind="ExternalInput").ap()
    d_wq = nc.dram_tensor("wq", [NH, 128, 2048], F32, kind="ExternalInput").ap()
    d_wk = nc.dram_tensor("wk", [NH, 128, 512], F32, kind="ExternalInput").ap()
    d_wv = nc.dram_tensor("wv", [NH, 128, 512], F32, kind="ExternalInput").ap()
    d_wkv = nc.dram_tensor("wkv", [128, 8192], F32, kind="ExternalInput").ap()
    d_wo = nc.dram_tensor("wo", [DC, 128, 2048], F32, kind="ExternalInput").ap()
    d_fc1 = nc.dram_tensor("fc1", [MC, 128, 2048], F32, kind="ExternalInput").ap()
    d_fc2 = nc.dram_tensor("fc2", [DC, 128, 1024], F32, kind="ExternalInput").ap()
    d_up = nc.dram_tensor("up", [SC, 128, 2048], F32, kind="ExternalInput").ap()
    d_dn = nc.dram_tensor("dn", [DC, 128, 4096], F32, kind="ExternalInput").ap()
    d_w1 = nc.dram_tensor("w1", [NE, 128, 8192], F32, kind="ExternalInput").ap()
    d_w2 = nc.dram_tensor("w2", [NE, 128, 8192], F32, kind="ExternalInput").ap()
    d_cwscr = nc.dram_tensor("cwscr", [NE, 1024], BF16, kind="Internal")
    d_y = nc.dram_tensor("y", [D, T], F32, kind="ExternalOutput").ap()
    d_dbg = {}
    for name, shape in dbg:
        d_dbg[name] = nc.dram_tensor("dbg_" + name, list(shape), F32,
                                     kind="ExternalOutput").ap()

    SBBASE = 16512

    def sb(name, shape, dtype, off):
        return nc.alloc_sbuf_tensor_at(name, shape, dtype, offset=SBBASE + off)

    acc = sb("acc", [128, DC * 1024], F32, 0)
    col1 = sb("col1", [128, 32], F32, 65536)
    ones_f = sb("ones_f", [128, 128], F32, 65664)
    ones_b = sb("ones_b", [128, 16], BF16, 66176)
    ident = sb("ident", [128, 128], BF16, 66208)
    gw = sb("gw", [128, 512], F32, 66464)
    biasb = sb("biasb", [128, 32], F32, 68512)
    rt = sb("rt", [128, 256], F32, 68640)
    cwtok = sb("cwtok", [128, 256], F32, 69664)
    masks = sb("masks", [128, NQT * 272], BF16, 70688)
    cwfm = sb("cwfm", [32, 1024], BF16, 75040)
    cwbf = sb("cwbf", [128, 32], BF16, 77088)
    AB = 77152
    AEND = 229376 - 16512
    x0b = sb("x0b", [128, DC * TH], BF16, AB + 0)
    kvb = sb("kvb", [128, LC * TH], BF16, AB + 37376)
    ao = sb("ao", [128, NH * 1024], BF16, AB + 46720)
    qh = [sb("qh0", [128, 1024], BF16, AB + 79488),
          sb("qh1", [128, 1024], BF16, AB + 126464)]
    kh = [sb("kh0", [128, TH], BF16, AB + 81536),
          sb("kh1", [128, TH], BF16, AB + 128512)]
    vh = [sb("vh0", [128, 1280], BF16, AB + 83872),
          sb("vh1", [128, 1280], BF16, AB + 130848)]
    sprob = [sb("sprob0", [128, 272], F32, AB + 86432),
             sb("sprob1", [128, 272], F32, AB + 133408)]
    pbf = [sb("pbf0", [128, 272], BF16, AB + 87520),
           sb("pbf1", [128, 272], BF16, AB + 134496)]
    pTe = sb("pTe", [128, 768], BF16, AB + 88064)
    wkh = [sb("wkh0", [128, 512], BF16, AB + 89600),
           sb("wkh1", [128, 512], BF16, AB + 90624)]
    wvh = [sb("wvh0", [128, 512], BF16, AB + 91648),
           sb("wvh1", [128, 512], BF16, AB + 92672)]
    wqh = [sb("wqh0", [128, 2048], BF16, AB + 93696),
           sb("wqh1", [128, 2048], BF16, AB + 97792)]
    woh = [sb("woh0", [128, 2048], BF16, AB + 101888),
           sb("woh1", [128, 2048], BF16, AB + 105984)]
    wkvb = sb("wkvb", [128, 8192], BF16, AB + 110080)
    assert AB + 135040 <= AEND, AB
    x1b = sb("x1b", [128, DC * 1024], BF16, AB + 0)
    rows = sb("rows", [1, 3072], F32, AB + 32768)
    sqbf = [sb("sqbf0", [128, 1024], BF16, AB + 45056),
            sb("sqbf1", [128, 1024], BF16, AB + 47104)]
    sq2bf = [sb("sq2bf0", [128, 1024], BF16, AB + 49152),
             sb("sq2bf1", [128, 1024], BF16, AB + 51200)]
    xlat = sb("xlat", [128, MC * 1024], BF16, AB + 53248)
    fc1s = [sb("fc1s0", [128, 2048], BF16, AB + 32768),
            sb("fc1s1", [128, 2048], BF16, AB + 36864)]
    ups = [sb("ups0", [128, 2048], BF16, AB + 32768),
           sb("ups1", [128, 2048], BF16, AB + 36864)]
    dns = [sb("dns0", [128, 1024], BF16, AB + 40960),
           sb("dns1", [128, 1024], BF16, AB + 43008)]
    interb = sb("interb", [128, 8 * 1024], BF16, AB + 69632)
    ubuf = [sb("ubuf0", [128, 512], F32, AB + 102400),
            sb("ubuf1", [128, 512], F32, AB + 104448)]
    h2b = [sb("h2b0", [128, 512], BF16, AB + 106496),
           sb("h2b1", [128, 512], BF16, AB + 107520)]
    yacc = sb("yacc", [128, MC * 1024], F32, AB + 0)
    hbf = sb("hbf", [128, MC * 1024], BF16, AB + 36864)
    cwb = [sb("cwb0", [128, 1024], BF16, AB + 108544),
           sb("cwb1", [128, 1024], BF16, AB + 110592)]
    w1s = [sb("w1s0", [128, 8192], BF16, AB + 86016),
           sb("w1s1", [128, 8192], BF16, AB + 69632)]
    w2s = sb("w2s", [128, 8192], BF16, AB + 112640)
    assert AB + 129024 <= AEND
    ylat = sb("ylat", [128, MC * 1024], BF16, AB + 36864)
    fc2s = [sb("fc2s0", [128, 1024], BF16, AB + 108544),
            sb("fc2s1", [128, 1024], BF16, AB + 110592)]
    outst = [sb("outst0", [128, 1024], F32, AB + 53248),
             sb("outst1", [128, 1024], F32, AB + 57344)]
    rows2 = sb("rows2", [1, 3072], F32, AB + 69632)
    sqbf2 = [sb("sqbf20", [128, 1024], BF16, AB + 81920),
             sb("sqbf21", [128, 1024], BF16, AB + 83968)]
    rbc1 = sb("rbc1", [128, 1024], F32, AB + 53248)
    mbc1 = sb("mbc1", [128, 1024], F32, AB + 57344)
    rbc2 = sb("rbc2", [128, 1024], F32, AB + 90112)
    mbc2 = sb("mbc2", [128, 1024], F32, AB + 94208)
    sq2bf2 = [sb("sq2bf20", [128, 1024], BF16, AB + 86016),
              sb("sq2bf21", [128, 1024], BF16, AB + 88064)]

    pA = nc.alloc_psum_tensor("pA", [128, 1024], F32)
    pB = nc.alloc_psum_tensor("pB", [128, 1024], F32)
    pC = nc.alloc_psum_tensor("pC", [128, 1024], F32)
    pT1 = nc.alloc_psum_tensor("pT1", [128, 512], BF16)
    pT2 = nc.alloc_psum_tensor("pT2", [128, 512], BF16)
    PS = {"pA": pA, "pB": pB, "pC": pC}

    def pwin(rn):
        t = PS[rn[:2]]
        half = int(rn[-1])
        return t[:, half * 512:(half + 1) * 512]

    B = Bld(nc)
    B.new_res(*[f"a{i}" for i in range(DC)])
    B.new_res("col1", "ones_f", "ones_b", "ident", "gw", "biasb", "rt",
              "cwtok", "masks", "cwfm", "cwbf",
              "x0b", "kvb", "ao", "qh0", "qh1", "kh0", "kh1", "vh0", "vh1",
              "sprob0", "sprob1", "pbf0", "pbf1", "colp0", "colp1", "pTe",
              "wkh0", "wkh1", "wvh0", "wvh1", "wqh0", "wqh1", "woh0", "woh1",
              "wkvb",
              "pA.0", "pA.1", "pB.0", "pB.1", "pC.0", "pC.1", "pT1", "pT2",
              "cwscr")

    def dma(stream, sem, out_ap, in_ap, reads=(), writes=()):
        def emit(e, o=out_ap, i=in_ap):
            return e.dma_start(o, i)
        return B.op(stream, emit, reads=reads, writes=writes, sem=sem, inc=16)

    def mmg(mms, reads, writes):
        def emit(t, mms=mms):
            ins = None
            for (o, l, r, st, sp) in mms:
                ins = t.matmul(o, l, r, start=st, stop=sp)
            return ins
        return B.op("tensor", emit, reads=reads, writes=writes)

    def scpy(out_ap, in_ap, reads, writes, func=AF.Copy, scale=1.0, bias=0.0):
        def emit(s, o=out_ap, i=in_ap, f=func, sc=scale, b=bias):
            return s.activation(o, i, f, bias=b, scale=sc)
        return B.op("scalar", emit, reads=reads, writes=writes)

    def vop(fn, reads, writes):
        return B.op("vector", fn, reads=reads, writes=writes)

    s_in_h = B.sem("s_in")

    def emit_in(e):
        e.dma_start(acc[:, :].rearrange("p (m t) -> p m t", m=DC),
                    d_xkv[:, 128:1152].rearrange("(m p) t -> p m t", p=128)
                    ).then_inc(s_in_h, 16)
        e.dma_start(masks[:, :].rearrange("p (j k) -> p j k", j=NQT),
                    d_mask.rearrange("j p k -> p j k")).then_inc(s_in_h, 16)
        e.dma_start(ident[:, :], d_ident).then_inc(s_in_h, 16)
        e.dma_start(biasb[:, :], d_biasb).then_inc(s_in_h, 16)
        return e.dma_start(gw[:, :], d_gw)
    B.semcnt["s_in"] += 4 * 16
    B.op("sync", emit_in,
         writes=tuple(f"a{i}" for i in range(DC)) + ("masks", "ident", "biasb", "gw"),
         sem="s_in", inc=16)

    dma("gpsimd", "s_x0b", x0b[:, :].rearrange("p (m t) -> p m t", m=DC),
        d_xkv.rearrange("(m p) t -> p m t", p=128), writes=("x0b",))
    dma("gpsimd", "s_wkv", wkvb[:, :], d_wkv, writes=("wkvb",))
    dma("gpsimd", "s_wq0", wqh[0][:, :], d_wq[0], writes=("wqh0",))
    dma("gpsimd", "s_wk0", wkh[0][:, :], d_wk[0], writes=("wkh0",))
    dma("gpsimd", "s_wv0", wvh[0][:, :], d_wv[0], writes=("wvh0",))

    vop(lambda v: v.memset(ones_f[:, :], 1.0), (), ("ones_f",))
    vop(lambda v: v.memset(ones_b[:, :], 1.0), (), ("ones_b",))
    vop(lambda v: v.memset(col1[:, 6:7], 1e-5), (), ("col1",))
    vop(lambda v: v.memset(col1[:, 7:8], -20.0), (), ("col1",))
    vop(lambda v: v.memset(rt[:, 76:80], -BIG), (), ("rt",))

    KWIN = [(0, 512), (512, 1024), (1024, TH)]
    rot6 = ["pA.0", "pA.1", "pB.0", "pB.1", "pC.0", "pC.1"]
    ri = 0
    for m in range(LC):
        for (w0, w1) in KWIN:
            rn = rot6[ri % 6]
            ri += 1
            mms = [(pwin(rn)[:, 0:w1 - w0],
                    wkvb[:, (fc * LC + m) * 128:(fc * LC + m) * 128 + 128],
                    x0b[:, fc * TH + w0:fc * TH + w1],
                    fc == 0, fc == DC - 1) for fc in range(DC)]
            mmg(mms, reads=("wkvb", "x0b"), writes=(rn,))
            scpy(kvb[:, m * TH + w0:m * TH + w1], pwin(rn)[:, 0:w1 - w0],
                 reads=(rn,), writes=("kvb",))

    def proj_tasks(h):
        sl = h % 2
        tasks = []

        def ktask(wi, w0, w1, sl=sl):
            rn = ["pA.0", "pA.1", "pA.0"][wi]
            mms = [(pwin(rn)[:, 0:w1 - w0],
                    wkh[sl][:, lc * 128:(lc + 1) * 128],
                    kvb[:, lc * TH + w0:lc * TH + w1],
                    lc == 0, lc == LC - 1) for lc in range(LC)]
            mmg(mms, reads=(f"wkh{sl}", "kvb"), writes=(rn,))
            scpy(kh[sl][:, w0:w1], pwin(rn)[:, 0:w1 - w0], reads=(rn,),
                 writes=(f"kh{sl}",))

        for wi, (w0, w1) in enumerate(KWIN):
            tasks.append(lambda wi=wi, w0=w0, w1=w1: ktask(wi, w0, w1))

        def vtask(tt, sl=sl):
            t0 = tt * 128
            tn = min(128, TH - t0)
            rn = ["pA.0", "pA.1"][tt % 2]
            mms = [(pwin(rn)[0:tn, 0:128],
                    kvb[:, lc * TH + t0:lc * TH + t0 + tn],
                    wvh[sl][:, lc * 128:(lc + 1) * 128],
                    lc == 0, lc == LC - 1) for lc in range(LC)]
            mmg(mms, reads=("kvb", f"wvh{sl}"), writes=(rn,))
            scpy(vh[sl][0:tn, t0:t0 + 128], pwin(rn)[0:tn, 0:128], reads=(rn,),
                 writes=(f"vh{sl}",))

        for tt in range(10):
            tasks.append(lambda tt=tt: vtask(tt))

        def qtask(wi, sl=sl):
            rn = ["pA.0", "pA.1"][wi]
            w0 = wi * 512
            mms = [(pwin(rn),
                    wqh[sl][:, fc * 128:(fc + 1) * 128],
                    x0b[:, fc * TH + 128 + w0:fc * TH + 128 + w0 + 512],
                    fc == 0, fc == DC - 1) for fc in range(DC)]
            mmg(mms, reads=(f"wqh{sl}", "x0b"), writes=(rn,))
            scpy(qh[sl][:, w0:w0 + 512], pwin(rn), reads=(rn,),
                 writes=(f"qh{sl}",), scale=SCALE)

        for wi in range(2):
            tasks.append(lambda wi=wi: qtask(wi))
        return tasks

    def emit_scores(h, qt):
        sl = h % 2
        rn = ["pB.0", "pB.1"][qt % 2]
        mms = [(pwin(rn)[:, 0:256], qh[sl][:, qt * 128:(qt + 1) * 128],
                kh[sl][:, qt * 128:qt * 128 + 256], True, True),
               (pwin(rn)[:, 256:272], qh[sl][:, qt * 128:(qt + 1) * 128],
                kh[sl][:, 1152:1168], True, True)]
        mmg(mms, reads=(f"qh{sl}", f"kh{sl}"), writes=(rn,))

    def emit_softmaxA(h, qt):
        p = qt % 2
        rn = ["pB.0", "pB.1"][p]
        sp = sprob[p]
        scpy(sp[:, :], pwin(rn)[:, 0:272], reads=(rn, "col1"),
             writes=(f"sprob{p}",), func=AF.Exp, bias=col1[:, 7:8])

    def emit_softmaxB(h, qt):
        p = qt % 2
        sp, pb_, cr = sprob[p], pbf[p], f"colp{p}"
        c0 = 3 * p
        vop(lambda v, qt=qt, sp=sp, c0=c0: v.scalar_tensor_tensor(
            sp[:, :], sp[:, :], 1.0,
            masks[:, qt * 272:(qt + 1) * 272], ALU.mult, ALU.mult,
            accum_out=col1[:, c0 + 1:c0 + 2]),
            reads=(f"sprob{p}", "masks"), writes=(f"sprob{p}", cr))
        vop(lambda v, c0=c0: v.reciprocal(col1[:, c0 + 2:c0 + 3],
                                          col1[:, c0 + 1:c0 + 2]),
            reads=(cr,), writes=(cr,))
        vop(lambda v, sp=sp, pb_=pb_, c0=c0: v.tensor_single_scalar(
            pb_[:, :], sp[:, :], col1[:, c0 + 2:c0 + 3], ALU.mult),
            reads=(f"sprob{p}", cr), writes=(f"pbf{p}",))

    def emit_avpart(h, qt):
        sl = h % 2
        p = qt % 2
        pb_ = pbf[p]
        ptn = ["pT1", "pT2"][p]
        pt = {"pT1": pT1, "pT2": pT2}[ptn]
        off = 384 * p

        def emt(t, pt=pt, pb_=pb_):
            t.transpose(pt[:, 0:128], pb_[:, 0:128], ident[:, :])
            t.transpose(pt[:, 128:256], pb_[:, 128:256], ident[:, :])
            return t.transpose(pt[0:16, 256:384], pb_[:, 256:272], ident[:, :])
        B.op("tensor", emt, reads=(f"pbf{p}", "ident"), writes=(ptn,))

        def eev(s, pt=pt, off=off):
            s.activation(pTe[:, off:off + 256], pt[:, 0:256], AF.Copy)
            return s.activation(pTe[0:16, off + 256:off + 384],
                                pt[0:16, 256:384], AF.Copy)
        B.op("scalar", eev, reads=(ptn,), writes=("pTe",))

        rn = ["pC.0", "pC.1"][qt % 2]
        mms = [(pwin(rn)[:, 0:128], vh[sl][:, qt * 128:(qt + 1) * 128],
                pTe[:, off:off + 128], True, False),
               (pwin(rn)[:, 0:128], vh[sl][:, (qt + 1) * 128:(qt + 2) * 128],
                pTe[:, off + 128:off + 256], False, False),
               (pwin(rn)[:, 0:128], vh[sl][0:16, 1152:1280],
                pTe[0:16, off + 256:off + 384], False, True)]
        mmg(mms, reads=(f"vh{sl}", "pTe"), writes=(rn,))
        scpy(ao[:, h * 1024 + qt * 128:h * 1024 + (qt + 1) * 128],
             pwin(rn)[:, 0:128], reads=(rn,), writes=("ao",))

    for t_ in proj_tasks(0):
        t_()
    for h in range(NH):
        if h + 1 < NH:
            nsl = (h + 1) % 2
            dma("gpsimd", f"s_wq{nsl}", wqh[nsl][:, :], d_wq[h + 1],
                writes=(f"wqh{nsl}",))
            dma("gpsimd", f"s_wk{nsl}", wkh[nsl][:, :], d_wk[h + 1],
                writes=(f"wkh{nsl}",))
            dma("gpsimd", f"s_wv{nsl}", wvh[nsl][:, :], d_wv[h + 1],
                writes=(f"wvh{nsl}",))
            nxt = proj_tasks(h + 1)
        else:
            nxt = []
        for qt in range(NQT):
            emit_scores(h, qt)
            emit_softmaxA(h, qt)
            if qt > 0:
                emit_softmaxB(h, qt - 1)
                emit_avpart(h, qt - 1)
            for _ in range(2):
                if nxt:
                    nxt.pop(0)()
        emit_softmaxB(h, NQT - 1)
        emit_avpart(h, NQT - 1)
        while nxt:
            nxt.pop(0)()

    dma("gpsimd", "s_wo0", woh[0][:, :], d_wo[0], writes=("woh0",))
    for m in range(DC):
        sl = m % 2
        if m + 1 < DC:
            nsl = (m + 1) % 2
            dma("gpsimd", f"s_wo{nsl}", woh[nsl][:, :], d_wo[m + 1],
                writes=(f"woh{nsl}",))
        for wi in range(2):
            rn = ["pA.0", "pA.1"][wi]
            w0 = wi * 512
            mms = [(pwin(rn),
                    woh[sl][:, c * 128:(c + 1) * 128],
                    ao[:, c * 1024 + w0:c * 1024 + w0 + 512],
                    c == 0, c == DC - 1) for c in range(DC)]
            mmg(mms, reads=(f"woh{sl}", "ao"), writes=(rn,))
            vop(lambda v, m=m, w0=w0, rn=rn: v.tensor_tensor(
                acc[:, m * 1024 + w0:m * 1024 + w0 + 512],
                acc[:, m * 1024 + w0:m * 1024 + w0 + 512], pwin(rn), ALU.add),
                reads=(rn, f"a{m}"), writes=(f"a{m}",))

    if "kvb" in d_dbg:
        dma("gpsimd", "s_dbgk", d_dbg["kvb"], kvb[:, :], reads=("kvb",))
        B.drain("gpsimd", "s_dbgk")
    if "ao" in d_dbg:
        dma("gpsimd", "s_dbga", d_dbg["ao"], ao[:, :], reads=("ao",))
        B.drain("gpsimd", "s_dbga")
    if "pre1" in d_dbg:
        dma("gpsimd", "s_dbgp", d_dbg["pre1"], acc[:, :],
            reads=tuple(f"a{i}" for i in range(DC)))
        B.drain("gpsimd", "s_dbgp")

    B.retire("x0b", "kvb", "ao", "qh", "kh", "vh", "sprob", "pbf", "pTe",
             "wkh0", "wkh1", "wvh0", "wvh1", "wqh0", "wqh1", "woh0", "woh1",
             "wkvb")

    def layernorm(rows_t, rows_rn, sqp, sq2p, sqrn, sq2rn, apply_fn,
                  rbc_t, mbc_t, bcrn):
        statbank = ["pA.0", "pA.1", "pB.0", "pB.1"]
        for m in range(DC):
            sl = m % 2
            if sl == 0:
                scpy(sqp[sl][:, :], acc[:, m * 1024:(m + 1) * 1024],
                     reads=(f"a{m}",), writes=(sqrn + str(sl),))
            else:
                vop(lambda v, m=m, sl=sl: v.tensor_copy(
                    sqp[sl][:, :], acc[:, m * 1024:(m + 1) * 1024]),
                    reads=(f"a{m}",), writes=(sqrn + str(sl),))
            for wi in range(2):
                rn = statbank[wi]
                mmg([(pwin(rn)[0:1, 0:512], ones_b[:, 0:1],
                      sqp[sl][:, wi * 512:wi * 512 + 512], m == 0, m == DC - 1)],
                    reads=(sqrn + str(sl), "ones_b"), writes=(rn,))
            if sl == 0:
                scpy(sq2p[sl][:, :], sqp[sl][:, :],
                     reads=(sqrn + str(sl),), writes=(sq2rn + str(sl),),
                     func=AF.Square)
            else:
                vop(lambda v, sl=sl: v.tensor_mul(
                    sq2p[sl][:, :], sqp[sl][:, :], sqp[sl][:, :]),
                    reads=(sqrn + str(sl),), writes=(sq2rn + str(sl),))
            for wi in range(2):
                rn = statbank[2 + wi]
                mmg([(pwin(rn)[0:1, 0:512], ones_b[:, 0:1],
                      sq2p[sl][:, wi * 512:wi * 512 + 512], m == 0, m == DC - 1)],
                    reads=(sq2rn + str(sl), "ones_b"), writes=(rn,))
        vop(lambda v: v.tensor_copy(rows_t[0:1, 0:1024], pA[0:1, 0:1024]),
            reads=("pA.0", "pA.1"), writes=(rows_rn,))
        vop(lambda v: v.scalar_tensor_tensor(
            rows_t[0:1, 2048:3072], rows_t[0:1, 0:1024], 1.0 / (D * D),
            rows_t[0:1, 0:1024], ALU.mult, ALU.mult),
            reads=(rows_rn,), writes=(rows_rn,))
        vop(lambda v: v.scalar_tensor_tensor(
            rows_t[0:1, 1024:2048], pB[0:1, 0:1024], 1.0 / D,
            rows_t[0:1, 2048:3072], ALU.mult, ALU.subtract),
            reads=("pB.0", "pB.1", rows_rn), writes=(rows_rn,))
        scpy(rows_t[0:1, 1024:2048], rows_t[0:1, 1024:2048],
             reads=(rows_rn, "col1"), writes=(rows_rn,), func=AF.Ln,
             bias=col1[0:1, 6:7])
        scpy(rows_t[0:1, 1024:2048], rows_t[0:1, 1024:2048], reads=(rows_rn,),
             writes=(rows_rn,), func=AF.Exp, scale=-0.5)
        vop(lambda v: v.scalar_tensor_tensor(
            rows_t[0:1, 2048:3072], rows_t[0:1, 0:1024], -1.0 / D,
            rows_t[0:1, 1024:2048], ALU.mult, ALU.mult),
            reads=(rows_rn,), writes=(rows_rn,))
        for wi in range(2):
            rb = ["pC.0", "pC.1"][wi]
            mb = ["pA.0", "pA.1"][wi]
            mmg([(pwin(rb), ones_f[0:1, 0:128],
                  rows_t[0:1, 1024 + wi * 512:1024 + wi * 512 + 512], True, True)],
                reads=(rows_rn, "ones_f"), writes=(rb,))
            mmg([(pwin(mb), ones_f[0:1, 0:128],
                  rows_t[0:1, 2048 + wi * 512:2048 + wi * 512 + 512], True, True)],
                reads=(rows_rn, "ones_f"), writes=(mb,))
        vop(lambda v: v.tensor_copy(rbc_t[:, :], pC[:, 0:1024]),
            reads=("pC.0", "pC.1"), writes=(bcrn + "r",))
        vop(lambda v: v.tensor_copy(mbc_t[:, :], pA[:, 0:1024]),
            reads=("pA.0", "pA.1"), writes=(bcrn + "m",))
        for m in range(DC):
            apply_fn(m)

    B.new_res("rows", "sqbf0", "sqbf1", "sq2bf0", "sq2bf1",
              *[f"xb{i}" for i in range(DC)])

    def ln1_apply(m):
        eng = "vector" if m < 11 else "gpsimd"
        B.op(eng, lambda e, m=m: e.tensor_tensor(
            acc[:, m * 1024:(m + 1) * 1024],
            acc[:, m * 1024:(m + 1) * 1024], rbc1[:, :], ALU.mult),
            reads=("b1r", f"a{m}"), writes=(f"a{m}",),
            sem="ve" if eng == "vector" else "gp")
        B.op(eng, lambda e, m=m: e.tensor_tensor(
            acc[:, m * 1024:(m + 1) * 1024],
            acc[:, m * 1024:(m + 1) * 1024], mbc1[:, :], ALU.add),
            reads=("b1m", f"a{m}"), writes=(f"a{m}",),
            sem="ve" if eng == "vector" else "gp")
        scpy(x1b[:, m * 1024:(m + 1) * 1024], acc[:, m * 1024:(m + 1) * 1024],
             reads=(f"a{m}",), writes=(f"xb{m}",))

    B.new_res("b1r", "b1m")
    layernorm(rows, "rows", sqbf, sq2bf, "sqbf", "sq2bf", ln1_apply, rbc1, mbc1, "b1")
    B.retire("b1r", "b1m")
    B.retire("rows", "sqbf0", "sqbf1", "sq2bf0", "sq2bf1")

    if "x1" in d_dbg:
        dma("gpsimd", "s_dbg1", d_dbg["x1"], acc[:, :],
            reads=tuple(f"a{i}" for i in range(DC)))
        B.drain("gpsimd", "s_dbg1")

    B.new_res("xlat", "fc1s0", "fc1s1")
    dma("gpsimd", "s_fc10", fc1s[0][:, :], d_fc1[0], writes=("fc1s0",))

    fc1_tasks = []
    for m in range(MC):
        for wi in range(2):
            def f1task(m=m, wi=wi):
                sl = m % 2
                if wi == 0 and m + 1 < MC:
                    nsl = (m + 1) % 2
                    dma("gpsimd", f"s_fc1{nsl}", fc1s[nsl][:, :], d_fc1[m + 1],
                        writes=(f"fc1s{nsl}",))
                rn = ["pA.0", "pA.1"][wi]
                w0 = wi * 512
                members = []
                for c in range(DC):
                    def fmm(t, c=c, w0=w0, sl=sl, rn=rn):
                        return t.matmul(pwin(rn),
                                        fc1s[sl][:, c * 128:(c + 1) * 128],
                                        x1b[:, c * 1024 + w0:c * 1024 + w0 + 512],
                                        start=(c == 0), stop=(c == DC - 1))
                    members.append((fmm, (f"fc1s{sl}", f"xb{c}")))
                B.member_ops("tensor", members, writes=(rn,))
                scpy(xlat[:, m * 1024 + w0:m * 1024 + w0 + 512], pwin(rn),
                     reads=(rn,), writes=("xlat",))
            fc1_tasks.append(f1task)

    def gate_tile(tt):
        grn = ["pB.0", "pB.1"][tt % 2]
        members = []
        for fc in range(DC):
            def gmm(t, fc=fc, grn=grn, tt=tt):
                return t.matmul(
                    pwin(grn)[:, 0:32],
                    acc[:, fc * 1024 + tt * 128:fc * 1024 + (tt + 1) * 128],
                    gw[:, fc * 32:(fc + 1) * 32],
                    start=(fc == 0), stop=(fc == DC - 1))
            members.append((gmm, (f"a{fc}", "gw")))
        B.member_ops("tensor", members, writes=(grn,))
        return grn

    for tt in range(NQT):
        grn = gate_tile(tt)
        scpy(rt[:, 200:232], pwin(grn)[:, 0:32], reads=(grn,), writes=("rt",),
             func=AF.Exp, scale=-1.0)
        vop(lambda v: v.tensor_scalar_add(rt[:, 200:232], rt[:, 200:232], 1.0),
            reads=("rt",), writes=("rt",))
        vop(lambda v: v.reciprocal(rt[:, 0:32], rt[:, 200:232]),
            reads=("rt",), writes=("rt",))
        vop(lambda v: v.tensor_add(rt[:, 32:64], rt[:, 0:32], biasb[:, :]),
            reads=("rt", "biasb"), writes=("rt",))
        for g in range(4):
            vop(lambda v, g=g: v.max(rt[:, 64:72], rt[:, 32 + 8 * g:40 + 8 * g]),
                reads=("rt",), writes=("rt",))
            vop(lambda v, g=g: v.tensor_add(rt[:, 72 + g:73 + g], rt[:, 64:65],
                                            rt[:, 65:66]),
                reads=("rt",), writes=("rt",))
        vop(lambda v: v.max(rt[:, 80:88], rt[:, 72:80]), reads=("rt",), writes=("rt",))
        vop(lambda v: v.tensor_single_scalar(rt[:, 88:92], rt[:, 72:76],
                                             rt[:, 81:82], ALU.is_ge),
            reads=("rt",), writes=("rt",))
        vop(lambda v: v.tensor_scalar(rt[:, 92:96], rt[:, 88:92], 1.0, BIG,
                                      ALU.subtract, ALU.mult),
            reads=("rt",), writes=("rt",))
        for g in range(4):
            vop(lambda v, g=g: v.tensor_single_scalar(
                rt[:, 96 + 8 * g:104 + 8 * g], rt[:, 32 + 8 * g:40 + 8 * g],
                rt[:, 92 + g:93 + g], ALU.add),
                reads=("rt",), writes=("rt",))
        vop(lambda v: v.max(rt[:, 128:136], rt[:, 96:128]), reads=("rt",),
            writes=("rt",))
        vop(lambda v: v.tensor_single_scalar(rt[:, 136:168], rt[:, 96:128],
                                             rt[:, 135:136], ALU.is_ge),
            reads=("rt",), writes=("rt",))
        vop(lambda v: v.tensor_mul(rt[:, 168:200], rt[:, 0:32], rt[:, 136:168]),
            reads=("rt",), writes=("rt",))
        vop(lambda v: v.reduce_sum(col1[:, 8:9], rt[:, 168:200], axis=AX.X),
            reads=("rt",), writes=("col1",))
        vop(lambda v: v.reciprocal(col1[:, 9:10], col1[:, 8:9]),
            reads=("col1",), writes=("col1",))
        vop(lambda v, tt=tt: v.tensor_scalar(cwtok[:, 32 * tt:32 * (tt + 1)],
                                             rt[:, 168:200], col1[:, 9:10], 2.5,
                                             ALU.mult, ALU.mult),
            reads=("rt", "col1"), writes=("cwtok",))

        for _ in range(2):
            if fc1_tasks:
                fc1_tasks.pop(0)()
    while fc1_tasks:
        fc1_tasks.pop(0)()
    if "cwtok" in d_dbg:
        dma("gpsimd", "s_dbg2", d_dbg["cwtok"], cwtok[:, :], reads=("cwtok",))
        B.drain("gpsimd", "s_dbg2")

    for tt in range(NQT):
        ptn = ["pT1", "pT2"][tt % 2]
        pt = {"pT1": pT1, "pT2": pT2}[ptn]
        vop(lambda v, tt=tt: v.tensor_copy(cwbf[:, :],
                                           cwtok[:, 32 * tt:32 * (tt + 1)]),
            reads=("cwtok",), writes=("cwbf",))

        def emt(t, pt=pt):
            return t.transpose(pt[0:32, 0:128], cwbf[:, :], ident[:, :])
        B.op("tensor", emt, reads=("cwbf", "ident"), writes=(ptn,))
        vop(lambda v, pt=pt, tt=tt: v.tensor_copy(
            cwfm[:, tt * 128:(tt + 1) * 128], pt[0:32, 0:128]),
            reads=(ptn,), writes=("cwfm",))
    dma("sync", "s_cwo", d_cwscr.ap(), cwfm[:, :], reads=("cwfm",),
        writes=("cwscr",))

    if "xlat" in d_dbg:
        dma("gpsimd", "s_dbg3", d_dbg["xlat"], xlat[:, :], reads=("xlat",))
        B.drain("gpsimd", "s_dbg3")

    B.retire("fc1s0", "fc1s1")
    B.new_res("ups0", "ups1", "dns0", "dns1", "ubuf0", "ubuf1",
              "h2b0", "h2b1", "w1s0", "w2s", "cwb0", "cwb1",
              *[f"ib{i}" for i in range(8)])
    for S4 in range(4):
        dma("gpsimd", "s_ups0", ups[0][:, :], d_up[8 * S4], writes=("ups0",))
        for mi in range(8):
            gi = 8 * S4 + mi
            sl = mi % 2
            if mi + 1 < 8:
                nsl = (mi + 1) % 2
                dma("gpsimd", f"s_ups{nsl}", ups[nsl][:, :], d_up[gi + 1],
                    writes=(f"ups{nsl}",))
            for wi in range(2):
                rn = ["pA.0", "pA.1"][wi]
                w0 = wi * 512
                mms = [(pwin(rn),
                        ups[sl][:, c * 128:(c + 1) * 128],
                        x1b[:, c * 1024 + w0:c * 1024 + w0 + 512],
                        c == 0, c == DC - 1) for c in range(DC)]
                mmg(mms, reads=(f"ups{sl}",) + tuple(f"xb{i}" for i in range(DC)),
                    writes=(rn,))
                usl = (mi * 2 + wi) % 2
                scpy(ubuf[usl][:, :], pwin(rn), reads=(rn,),
                     writes=(f"ubuf{usl}",), func=AF.Relu)
                scpy(interb[:, mi * 1024 + w0:mi * 1024 + w0 + 512],
                     ubuf[usl][:, :], reads=(f"ubuf{usl}",),
                     writes=(f"ib{mi}",), func=AF.Square)
        if S4 == 1:
            dma("gpsimd", "s_w10", w1s[0][:, 0:4096],
                d_w1[0][:, 0:4096], writes=("w1s0",))
        elif S4 == 2:
            dma("gpsimd", "s_w10", w1s[0][:, 4096:8192],
                d_w1[0][:, 4096:8192], writes=("w1s0",))
            dma("gpsimd", "s_w2", w2s[:, 0:4096],
                d_w2[0][:, 0:4096], writes=("w2s",))
        elif S4 == 3:
            dma("gpsimd", "s_w2", w2s[:, 4096:8192],
                d_w2[0][:, 4096:8192], writes=("w2s",))
        dma("gpsimd", "s_dns0", dns[0][:, :],
            d_dn[0][:, 1024 * S4:1024 * S4 + 1024], writes=("dns0",))
        for mo in range(DC):
            sl = mo % 2
            if mo + 1 < DC:
                nsl = (mo + 1) % 2
                dma("gpsimd", f"s_dns{nsl}", dns[nsl][:, :],
                    d_dn[mo + 1][:, 1024 * S4:1024 * S4 + 1024],
                    writes=(f"dns{nsl}",))
            rny = ["pB.0", "pC.0"][mo % 2]
            rny2 = ["pB.1", "pC.1"][mo % 2]
            members = []
            for cl in range(8):
                def dmm(t, cl=cl, sl=sl, rny=rny, rny2=rny2):
                    t.matmul(pwin(rny), dns[sl][:, cl * 128:(cl + 1) * 128],
                             interb[:, cl * 1024 + 0:cl * 1024 + 512],
                             start=(cl == 0), stop=(cl == 7))
                    return t.matmul(pwin(rny2),
                                    dns[sl][:, cl * 128:(cl + 1) * 128],
                                    interb[:, cl * 1024 + 512:cl * 1024 + 1024],
                                    start=(cl == 0), stop=(cl == 7))
                members.append((dmm, (f"dns{sl}", f"ib{cl}")))
            B.member_ops("tensor", members, writes=(rny, rny2))
            pfull = PS[rny[:2]]
            vop(lambda v, mo=mo, pfull=pfull: v.tensor_tensor(
                acc[:, mo * 1024:(mo + 1) * 1024],
                acc[:, mo * 1024:(mo + 1) * 1024], pfull[:, :], ALU.add),
                reads=(rny, rny2, f"a{mo}"), writes=(f"a{mo}",))
        if S4 == 0:
            dma("sync", "s_cwb0", cwb[0][:, :],
                bass.AP(d_cwscr, 0, [[0, 128], [1, 1024]]), reads=("cwscr",),
                writes=("cwb0",))

    B.retire("ups0", "ups1", "dns0", "dns1",
             *[f"ib{i}" for i in range(8)], *[f"xb{i}" for i in range(DC)])

    B.new_res("yacc", "w1s1", *[f"hbf.{i}" for i in range(MC)])
    for e in range(NE):
        sl = e % 2
        if e + 1 < NE:
            nsl = (e + 1) % 2
            dma("gpsimd", f"s_w1{nsl}", w1s[nsl][:, :], d_w1[e + 1],
                writes=(f"w1s{nsl}",))
            dma("sync", f"s_cwb{nsl}", cwb[nsl][:, :],
                bass.AP(d_cwscr, (e + 1) * 1024, [[0, 128], [1, 1024]]),
                reads=("cwscr",), writes=(f"cwb{nsl}",))
        for mi in range(MC):
            for wi in range(2):
                par = (mi * 2 + wi) % 2
                rn = ["pA.0", "pA.1"][par]
                w0 = wi * 512
                mms = [(pwin(rn),
                        w1s[sl][:, lc * 1024 + mi * 128:lc * 1024 + (mi + 1) * 128],
                        xlat[:, lc * 1024 + w0:lc * 1024 + w0 + 512],
                        lc == 0, lc == MC - 1) for lc in range(MC)]
                mmg(mms, reads=(f"w1s{sl}", "xlat"), writes=(rn,))
                scpy(ubuf[par][:, :], pwin(rn), reads=(rn,),
                     writes=(f"ubuf{par}",), func=AF.Relu)
                scpy(h2b[par][:, :], ubuf[par][:, :], reads=(f"ubuf{par}",),
                     writes=(f"h2b{par}",), func=AF.Square)
                vop(lambda v, mi=mi, w0=w0, par=par, sl=sl: v.tensor_tensor(
                    hbf[:, mi * 1024 + w0:mi * 1024 + w0 + 512],
                    h2b[par][:, :], cwb[sl][:, w0:w0 + 512], ALU.mult),
                    reads=(f"h2b{par}", f"cwb{sl}"), writes=(f"hbf.{mi}",))
        for ml in range(MC):
            rny = ["pB.0", "pC.0"][ml % 2]
            rny2 = ["pB.1", "pC.1"][ml % 2]
            if ml == 0:
                members = []
                for ic in range(MC):
                    def w2mm(t, ic=ic, ml=ml, rny=rny, rny2=rny2):
                        t.matmul(pwin(rny),
                                 w2s[:, ic * 1024 + ml * 128:ic * 1024 + (ml + 1) * 128],
                                 hbf[:, ic * 1024 + 0:ic * 1024 + 512],
                                 start=(ic == 0), stop=(ic == MC - 1))
                        return t.matmul(pwin(rny2),
                                        w2s[:, ic * 1024 + ml * 128:ic * 1024 + (ml + 1) * 128],
                                        hbf[:, ic * 1024 + 512:ic * 1024 + 1024],
                                        start=(ic == 0), stop=(ic == MC - 1))
                    members.append((w2mm, ("w2s", f"hbf.{ic}")))
                B.member_ops("tensor", members, writes=(rny, rny2))
            else:
                mms = []
                for ic in range(MC):
                    mms.append((pwin(rny),
                                w2s[:, ic * 1024 + ml * 128:ic * 1024 + (ml + 1) * 128],
                                hbf[:, ic * 1024 + 0:ic * 1024 + 512],
                                ic == 0, ic == MC - 1))
                for ic in range(MC):
                    mms.append((pwin(rny2),
                                w2s[:, ic * 1024 + ml * 128:ic * 1024 + (ml + 1) * 128],
                                hbf[:, ic * 1024 + 512:ic * 1024 + 1024],
                                ic == 0, ic == MC - 1))
                mmg(mms, reads=tuple(f"hbf.{i}" for i in range(MC)) + ("w2s",),
                    writes=(rny, rny2))
            pfull = PS[rny[:2]]
            if e == 0:
                vop(lambda v, ml=ml, pfull=pfull: v.tensor_copy(
                    yacc[:, ml * 1024:(ml + 1) * 1024], pfull[:, :]),
                    reads=(rny, rny2), writes=("yacc",))
            else:
                vop(lambda v, ml=ml, pfull=pfull: v.tensor_tensor(
                    yacc[:, ml * 1024:(ml + 1) * 1024],
                    yacc[:, ml * 1024:(ml + 1) * 1024], pfull[:, :], ALU.add),
                    reads=(rny, rny2, "yacc"), writes=("yacc",))
        if e + 1 < NE:
            dma("gpsimd", "s_w2", w2s[:, :], d_w2[e + 1], writes=("w2s",))
    if "yacc" in d_dbg:
        dma("gpsimd", "s_dbg4", d_dbg["yacc"], yacc[:, :], reads=("yacc",))
        B.drain("gpsimd", "s_dbg4")

    B.retire("cwb0", "cwb1", "w1s0", "w1s1", "w2s", "xlat",
             *[f"hbf.{i}" for i in range(MC)])
    B.new_res("fc2s0", "fc2s1", *[f"yl{i}" for i in range(MC)])
    for ml in range(MC):
        scpy(ylat[:, ml * 1024:(ml + 1) * 1024],
             yacc[:, ml * 1024:(ml + 1) * 1024], reads=("yacc",),
             writes=(f"yl{ml}",))
    dma("gpsimd", "s_fc20", fc2s[0][:, :], d_fc2[0], writes=("fc2s0",))
    for mo in range(DC):
        sl = mo % 2
        if mo + 1 < DC:
            nsl = (mo + 1) % 2
            dma("gpsimd", f"s_fc2{nsl}", fc2s[nsl][:, :], d_fc2[mo + 1],
                writes=(f"fc2s{nsl}",))
        rny = ["pB.0", "pC.0"][mo % 2]
        rny2 = ["pB.1", "pC.1"][mo % 2]
        members = []
        for lc in range(MC):
            def f2mm(t, lc=lc, sl=sl, rny=rny, rny2=rny2):
                t.matmul(pwin(rny), fc2s[sl][:, lc * 128:(lc + 1) * 128],
                         ylat[:, lc * 1024 + 0:lc * 1024 + 512],
                         start=(lc == 0), stop=(lc == MC - 1))
                return t.matmul(pwin(rny2), fc2s[sl][:, lc * 128:(lc + 1) * 128],
                                ylat[:, lc * 1024 + 512:lc * 1024 + 1024],
                                start=(lc == 0), stop=(lc == MC - 1))
            members.append((f2mm, (f"fc2s{sl}", f"yl{lc}")))
        B.member_ops("tensor", members, writes=(rny, rny2))
        pfull = PS[rny[:2]]
        vop(lambda v, mo=mo, pfull=pfull: v.tensor_tensor(
            acc[:, mo * 1024:(mo + 1) * 1024],
            acc[:, mo * 1024:(mo + 1) * 1024], pfull[:, :], ALU.add),
            reads=(rny, rny2, f"a{mo}"), writes=(f"a{mo}",))

    B.retire("fc2s0", "fc2s1", "yacc", *[f"yl{i}" for i in range(MC)])
    B.new_res("rows2", "sqbf20", "sqbf21", "sq2bf20", "sq2bf21",
              "outst0", "outst1", "y")

    def ln2_apply(m):
        osl = m % 2
        vop(lambda v, m=m, osl=osl: v.tensor_tensor(
            outst[osl][:, :], acc[:, m * 1024:(m + 1) * 1024],
            rbc2[:, :], ALU.mult),
            reads=("b2r", f"a{m}"), writes=(f"outst{osl}",))
        vop(lambda v, osl=osl: v.tensor_tensor(
            outst[osl][:, :], outst[osl][:, :], mbc2[:, :], ALU.add),
            reads=("b2m",), writes=(f"outst{osl}",))
        dma("sync", "s_y", d_y[m * 128:(m + 1) * 128, :], outst[osl][:, :],
            reads=(f"outst{osl}",), writes=("y",))

    B.new_res("b2r", "b2m")
    layernorm(rows2, "rows2", sqbf2, sq2bf2, "sqbf2", "sq2bf2", ln2_apply,
              rbc2, mbc2, "b2")
    B.drain("sync", "s_y")

    with nc.Block() as block:
        B.replay(block)
    return nc


def _prep_weights(i):
    f = np.ascontiguousarray
    w = {}
    w["wq"] = f(i["L0_attn_Wq"].reshape(16, 128, 16, 128).transpose(2, 1, 0, 3)
                .reshape(16, 128, 2048))
    w["wk"] = f(i["L0_attn_Wk"].reshape(4, 128, 16, 128).transpose(2, 1, 0, 3)
                .reshape(16, 128, 512))
    w["wv"] = f(i["L0_attn_Wv"].reshape(4, 128, 16, 128).transpose(2, 1, 0, 3)
                .reshape(16, 128, 512))
    w["wkv"] = f(i["L0_attn_Wkv"].reshape(16, 128, 4, 128).transpose(1, 0, 2, 3)
                 .reshape(128, 8192))
    w["wo"] = f(i["L0_attn_Wo"].reshape(16, 128, 16, 128).transpose(2, 1, 0, 3)
                .reshape(16, 128, 2048))
    w["gw"] = f(i["L0_ffn_gate_w"].reshape(32, 16, 128).transpose(2, 1, 0)
                .reshape(128, 512))
    w["biasb"] = f(np.broadcast_to(i["L0_ffn_gate_bias"], (128, 32)))
    w["fc1"] = f(i["L0_ffn_fc1_w"].reshape(8, 128, 16, 128).transpose(0, 3, 2, 1)
                 .reshape(8, 128, 2048))
    w["fc2"] = f(i["L0_ffn_fc2_w"].reshape(16, 128, 8, 128).transpose(0, 3, 2, 1)
                 .reshape(16, 128, 1024))
    w["up"] = f(i["L0_ffn_shared_up_w"].reshape(32, 128, 16, 128)
                .transpose(0, 3, 2, 1).reshape(32, 128, 2048))
    w["dn"] = f(i["L0_ffn_shared_down_w"].reshape(16, 128, 32, 128)
                .transpose(0, 3, 2, 1).reshape(16, 128, 4096))
    w["w1"] = f(i["L0_ffn_w1"].reshape(32, 1024, 8, 128).transpose(0, 3, 2, 1)
                .reshape(32, 128, 8192))
    w["w2"] = f(i["L0_ffn_w2"].reshape(32, 1024, 8, 128).transpose(0, 3, 2, 1)
                .reshape(32, 128, 8192))
    w["ident"] = np.eye(128, dtype=np.float32).astype(ml_dtypes.bfloat16)
    return w


def _masks_for_core(c):
    out = np.zeros((NQT, 128, 272), dtype=np.float32)
    j = np.arange(NQT)[:, None, None]
    a = np.arange(128)[None, :, None]
    qt = 1024 * c + 128 * j + a
    b = np.arange(256)[None, None, :]
    kt = 1024 * c - 128 + 128 * j + b
    ok_main = (kt >= 0) & (kt <= qt) & (((qt - kt) < 128) | (kt < 16))
    out[:, :, 0:256] = np.where(ok_main, 1.0, 0.0)
    bp = np.arange(16)[None, None, :]
    kstart = 1024 * c - 128 + 128 * j
    in_main = (bp >= kstart) & (bp < kstart + 256)
    ok_p = (bp <= qt) & (~in_main)
    out[:, :, 256:272] = np.where(ok_p, 1.0, 0.0)
    return out.astype(ml_dtypes.bfloat16)


def _xkv_for_core(x, c):
    own = x[1024 * c:1024 * (c + 1)]
    if c == 0:
        halo = np.zeros((128, D), dtype=np.float32)
    else:
        halo = x[1024 * c - 128:1024 * c]
    persist = x[0:16]
    blk = np.concatenate([halo, own, persist], axis=0)
    return np.ascontiguousarray(blk.T)


def kernel(**inputs):
    x = np.asarray(inputs["L0_attn_x"], dtype=np.float32).reshape(8192, 2048)
    w = _prep_weights({k: np.asarray(v, dtype=np.float32)
                       for k, v in inputs.items() if k != "L0_attn_x"})
    nc = build_nc()
    in_maps = []
    for c in range(NCORES):
        m = dict(w)
        m["xkv"] = _xkv_for_core(x, c)
        m["mask"] = _masks_for_core(c)
        in_maps.append(m)
    res = run_bass_kernel_spmd(nc, in_maps, core_ids=list(range(NCORES)))
    out = np.empty((8192, 2048), dtype=np.float32)
    for c in range(NCORES):
        out[1024 * c:1024 * (c + 1), :] = res.results[c]["y"].T
    return out.reshape(1, 8192, 2048)
